# version 04
# baseline (speedup 1.0000x reference)
import contextlib
import numpy as np
import ml_dtypes
import concourse.bass as bass
import concourse.mybir as mybir
from concourse.bass_utils import run_bass_kernel_spmd

F32 = mybir.dt.float32
BF16 = mybir.dt.bfloat16
AF = mybir.ActivationFunctionType
ALU = mybir.AluOpType
AX = mybir.AxisListType

B, S, H, NH, D, INTER = 2, 4096, 2048, 16, 128, 5632
TPC = 1024
KCH = H // 128
OCH = H // 128
ICH = INTER // 128
EPS = 1e-5
NEGM = -30000.0
GROUPS = [[0, 1, 2, 3], [4, 5, 6, 7]]


class Ev:
    __slots__ = ("eng", "val")

    def __init__(self, eng, val):
        self.eng = eng
        self.val = val


class Prog:
    ENGINES = ("tensor", "vector", "scalar", "gpsimd", "sync")

    def __init__(self, sems, base, barrier=False):
        self.sems = sems
        self.ops = {e: [] for e in self.ENGINES}
        self.counts = dict(base)
        self.barrier_evs = tuple(
            Ev(e, v) for e, v in base.items() if v > 0) if barrier else ()

    def add(self, eng, fn, wait=(), mark=False, dma=False):
        waits = tuple(w for w in wait if w is not None)
        if not self.ops[eng] and self.barrier_evs:
            waits += tuple(w for w in self.barrier_evs if w.eng != eng)
        ev = None
        if mark:
            self.counts[eng] += 16 if dma else 1
            ev = Ev(eng, self.counts[eng])
        self.ops[eng].append((fn, waits, ev, dma))
        return ev

    def emit(self, block):
        for eng_name in self.ENGINES:
            ops = self.ops[eng_name]
            if not ops:
                continue

            def body(engine, ops=ops, eng_name=eng_name):
                last_wait = {}
                n_since_drain = 0
                for fn, waits, ev, dma in ops:
                    need_drain = False
                    for w in waits:
                        if w.eng == eng_name:
                            need_drain = True
                            continue
                        if last_wait.get(w.eng, -1) >= w.val:
                            continue
                        last_wait[w.eng] = w.val
                        engine.wait_ge(self.sems[w.eng], w.val)
                    if need_drain and n_since_drain > 0:
                        engine.drain()
                        n_since_drain = 0
                    instr = fn(engine)
                    n_since_drain += 1
                    if ev is not None:
                        instr.then_inc(self.sems[eng_name], 16 if dma else 1)

            getattr(block, eng_name)(body)


def build_nc():
    nc = bass.Bass("TRN2", target_bir_lowering=False, detect_race_conditions=False)

    x_in = nc.dram_tensor("x", [TPC, H], F32, kind="ExternalInput").ap()
    wq = nc.dram_tensor("wq", [OCH, 128, KCH * 128], F32, kind="ExternalInput").ap()
    wk = nc.dram_tensor("wk", [OCH, 128, KCH * 128], F32, kind="ExternalInput").ap()
    wv = nc.dram_tensor("wv", [OCH, 128, KCH * 128], F32, kind="ExternalInput").ap()
    wo = nc.dram_tensor("wo", [KCH, 128, H], F32, kind="ExternalInput").ap()
    wg = nc.dram_tensor("wg", [ICH, 128, KCH * 128], F32, kind="ExternalInput").ap()
    wu = nc.dram_tensor("wu", [ICH, 128, KCH * 128], F32, kind="ExternalInput").ap()
    wd = nc.dram_tensor("wd", [OCH, 128, ICH * 128], F32, kind="ExternalInput").ap()
    bq_in = nc.dram_tensor("bq", [128, OCH], F32, kind="ExternalInput").ap()
    bk_in = nc.dram_tensor("bk", [128, OCH], F32, kind="ExternalInput").ap()
    bv_in = nc.dram_tensor("bv", [128, OCH], F32, kind="ExternalInput").ap()
    bo_in = nc.dram_tensor("bo_bcast", [128, H], F32, kind="ExternalInput").ap()
    gs_in = nc.dram_tensor("gs", [128, ICH], F32, kind="ExternalInput").ap()
    us_in = nc.dram_tensor("us", [128, ICH], F32, kind="ExternalInput").ap()
    ds_in = nc.dram_tensor("ds", [128, OCH], F32, kind="ExternalInput").ap()
    cosq_in = nc.dram_tensor("cosq", [128, TPC], BF16, kind="ExternalInput").ap()
    sinq_in = nc.dram_tensor("sinq", [128, TPC], BF16, kind="ExternalInput").ap()
    cosk_in = nc.dram_tensor("cosk", [128, TPC], BF16, kind="ExternalInput").ap()
    sink_in = nc.dram_tensor("sink", [128, TPC], BF16, kind="ExternalInput").ap()
    ident_in = nc.dram_tensor("ident", [128, 128], BF16, kind="ExternalInput").ap()
    identf_in = nc.dram_tensor("identf", [128, 128], F32, kind="ExternalInput").ap()
    rswap_in = nc.dram_tensor("rswap", [128, 128], BF16, kind="ExternalInput").ap()
    mask_in = nc.dram_tensor("mask", [128, 512], F32, kind="ExternalInput").ap()
    y_out = nc.dram_tensor("y", [H, TPC], F32, kind="ExternalOutput").ap()

    x2_dram = nc.dram_tensor("x2_dram", [TPC, H], F32)
    x2t_dram = nc.dram_tensor("x2t_dram", [H, TPC], F32)
    cck_in = nc.dram_tensor("cck_in", [H, TPC], BF16)
    ccv_in = nc.dram_tensor("ccv_in", [H, TPC], BF16)
    cck_out = nc.dram_tensor("cck_out", [4 * H, TPC], BF16)
    ccv_out = nc.dram_tensor("ccv_out", [4 * H, TPC], BF16)

    with contextlib.ExitStack() as outer:
        e = outer.enter_context
        sems = {n: e(nc.semaphore("sem_" + n)) for n in Prog.ENGINES}
        counts = {n: 0 for n in Prog.ENGINES}

        ident = e(nc.sbuf_tensor("ident_sb", [128, 128], BF16))
        identf = e(nc.sbuf_tensor("identf_sb", [128, 128], F32))
        rsw = e(nc.sbuf_tensor("rsw_sb", [128, 128], BF16))
        mask = e(nc.sbuf_tensor("mask_sb", [128, 512], F32))
        cosq = e(nc.sbuf_tensor("cosq_sb", [128, TPC], BF16))
        sinq = e(nc.sbuf_tensor("sinq_sb", [128, TPC], BF16))
        cosk = e(nc.sbuf_tensor("cosk_sb", [128, TPC], BF16))
        sink = e(nc.sbuf_tensor("sink_sb", [128, TPC], BF16))
        bo_sb = e(nc.sbuf_tensor("bo_sb", [128, H], F32))
        bq_sb = e(nc.sbuf_tensor("bq_sb", [128, OCH], F32))
        bk_sb = e(nc.sbuf_tensor("bk_sb", [128, OCH], F32))
        bv_sb = e(nc.sbuf_tensor("bv_sb", [128, OCH], F32))
        gs_sb = e(nc.sbuf_tensor("gs_sb", [128, ICH], F32))
        us_sb = e(nc.sbuf_tensor("us_sb", [128, ICH], F32))
        ds_sb = e(nc.sbuf_tensor("ds_sb", [128, OCH], F32))
        wstage = e(nc.sbuf_tensor("wstage", [128, 2048], F32))
        wbf = [e(nc.sbuf_tensor(f"wbf{i}", [128, 2048], BF16)) for i in range(2)]
        wbf2 = [e(nc.sbuf_tensor(f"wbf2_{i}", [128, 2048], BF16)) for i in range(2)]
        st1 = e(nc.sbuf_tensor("st1", [128, 16], F32))
        st2 = e(nc.sbuf_tensor("st2", [128, 16], F32))
        st3 = e(nc.sbuf_tensor("st3", [128, 16], F32))
        st4 = e(nc.sbuf_tensor("st4", [128, 16], F32))

        with contextlib.ExitStack() as sc_at:
            a_t = sc_at.enter_context(nc.sbuf_tensor("a_t", [128, NH * TPC], BF16))

            with contextlib.ExitStack() as sc_qt:
                qt_sb = sc_qt.enter_context(nc.sbuf_tensor("qt_sb", [128, NH * TPC], BF16))

                with contextlib.ExitStack() as sa:
                    ea = sa.enter_context
                    xres = [ea(nc.sbuf_tensor(f"xres{i}", [128, H], F32)) for i in range(2)]
                    xn_tok = ea(nc.sbuf_tensor("xn_tok", [128, H], BF16))
                    sq_scr = ea(nc.sbuf_tensor("sq_scr", [128, H], F32))
                    xn_t = ea(nc.sbuf_tensor("xn_t", [128, KCH * TPC], BF16))
                    qk_f = ea(nc.sbuf_tensor("qk_f", [128, TPC], BF16))
                    rp1 = ea(nc.sbuf_tensor("rp1", [128, TPC], BF16))
                    rp2 = ea(nc.sbuf_tensor("rp2", [128, TPC], BF16))
                    kv_st = ea(nc.sbuf_tensor("kv_st", [128, TPC], BF16))
                    ps_tr = ea(nc.psum_tensor("ps_tr", [128, 1024], BF16))
                    ps_p = [ea(nc.psum_tensor(f"ps_p{i}", [128, 512], F32)) for i in range(2)]
                    ps_rope = ea(nc.psum_tensor("ps_rope", [128, 1024], F32))

                    p = Prog(sems, counts)
                    AD = {}

                    def _cl(dst, src):
                        return p.add("gpsimd", lambda g, d=dst, s=src: g.dma_start(out=d, in_=s),
                                     mark=True, dma=True)

                    _cl(ident[:], ident_in)
                    _cl(identf[:], identf_in)
                    _cl(rsw[:], rswap_in)
                    _cl(mask[:], mask_in)
                    _cl(cosq[:], cosq_in)
                    _cl(sinq[:], sinq_in)
                    _cl(cosk[:], cosk_in)
                    _cl(sink[:], sink_in)
                    _cl(bo_sb[:], bo_in)
                    _cl(bq_sb[:], bq_in)
                    _cl(bk_sb[:], bk_in)
                    _cl(bv_sb[:], bv_in)
                    _cl(gs_sb[:], gs_in)
                    _cl(us_sb[:], us_in)
                    ev_consts = _cl(ds_sb[:], ds_in)

                    ev_evac = {}
                    tr_last = {}
                    for t in range(8):
                        xb = xres[t % 2]
                        ev_x = p.add("sync", lambda s, t=t, xb=xb: s.dma_start(
                            out=xb[:], in_=x_in[t * 128:(t + 1) * 128, :]),
                            wait=(AD.get(("xres_free", t % 2)),), mark=True, dma=True)
                        ev_sum = p.add("vector", lambda v, t=t, xb=xb: v.reduce_sum(
                            st1[:, t:t + 1], xb[:], axis=AX.X), wait=(ev_x,), mark=True)
                        ev_sq = p.add("scalar", lambda s, t=t, xb=xb: s.activation(
                            sq_scr[:], xb[:], AF.Square, accum_out=st2[:, t:t + 1]),
                            wait=(ev_x,), mark=True)
                        ev1 = p.add("vector", lambda v, t=t: v.tensor_scalar(
                            st1[:, t:t + 1], st1[:, t:t + 1], 1.0 / H, None, ALU.mult),
                            wait=(ev_sum,), mark=True)
                        ev2 = p.add("vector", lambda v, t=t: v.tensor_scalar(
                            st2[:, t:t + 1], st2[:, t:t + 1], 1.0 / H, None, ALU.mult),
                            wait=(ev_sq,), mark=True)
                        ev3 = p.add("vector", lambda v, t=t: v.tensor_mul(
                            st3[:, t:t + 1], st1[:, t:t + 1], st1[:, t:t + 1]),
                            wait=(ev1,), mark=True)
                        ev4 = p.add("vector", lambda v, t=t: v.tensor_sub(
                            st2[:, t:t + 1], st2[:, t:t + 1], st3[:, t:t + 1]),
                            wait=(ev2, ev3), mark=True)
                        ev5 = p.add("vector", lambda v, t=t: v.tensor_scalar(
                            st2[:, t:t + 1], st2[:, t:t + 1], EPS, None, ALU.add),
                            wait=(ev4,), mark=True)
                        ev6 = p.add("vector", lambda v, t=t: v.reciprocal(
                            st3[:, t:t + 1], st2[:, t:t + 1]), wait=(ev5,), mark=True)
                        ev7 = p.add("scalar", lambda s, t=t: s.activation(
                            st2[:, t:t + 1], st3[:, t:t + 1], AF.Sqrt),
                            wait=(ev6,), mark=True)
                        ev8 = p.add("vector", lambda v, t=t: v.tensor_mul(
                            st3[:, t:t + 1], st1[:, t:t + 1], st2[:, t:t + 1]),
                            wait=(ev7,), mark=True)
                        ev9 = p.add("vector", lambda v, t=t: v.tensor_scalar(
                            st3[:, t:t + 1], st3[:, t:t + 1], -1.0, None, ALU.mult),
                            wait=(ev8,), mark=True)
                        ev_norm = p.add("scalar", lambda s, t=t, xb=xb: s.activation(
                            xn_tok[:], xb[:], AF.Identity,
                            bias=st3[:, t:t + 1], scale=st2[:, t:t + 1]),
                            wait=(ev9, tr_last.get(t - 1)), mark=True)
                        AD[("xres_free", t % 2)] = ev_norm
                        for half in range(2):
                            ev_tr = None
                            for kk in range(8):
                                kc = half * 8 + kk
                                ev_tr = p.add("tensor", lambda te, kc=kc, kk=kk: te.transpose(
                                    ps_tr[:, kk * 128:(kk + 1) * 128],
                                    xn_tok[:, kc * 128:(kc + 1) * 128], ident[:]),
                                    wait=(ev_norm, ev_consts, AD.get(("pstr_free",))),
                                    mark=(kk == 7))
                            ev_ev = p.add("vector", lambda v, t=t, half=half: v.tensor_copy(
                                xn_t[:].rearrange("q (kc tt) -> q kc tt", kc=KCH)[
                                    :, half * 8:(half + 1) * 8, t * 128:(t + 1) * 128],
                                ps_tr[:].rearrange("q (kk r) -> q kk r", kk=8)),
                                wait=(ev_tr,), mark=True)
                            AD[("pstr_free",)] = ev_ev
                            ev_evac[(t, half)] = ev_ev
                            if half == 1:
                                tr_last[t] = ev_tr
                    ev_xnt = ev_evac[(7, 1)]

                    def proj(name, wsrc, bias_sb, kind):
                        ev_last = None
                        for oc in range(OCH):
                            buf = wbf[oc % 2]
                            ev_w = p.add("sync", lambda s, wsrc=wsrc, oc=oc: s.dma_start(
                                out=wstage[:], in_=wsrc[oc]),
                                wait=(AD.get(("wstage_free",)),), mark=True, dma=True)
                            ev_cast = p.add("vector", lambda v, buf=buf: v.tensor_copy(
                                buf[:], wstage[:]),
                                wait=(ev_w, AD.get(("wbf_free", oc % 2))), mark=True)
                            AD[("wstage_free",)] = ev_cast
                            ev_mm = [None, None]
                            for t2 in range(2):
                                for kc in range(KCH):
                                    ev_mm[t2] = p.add("tensor",
                                        lambda te, buf=buf, kc=kc, t2=t2: te.matmul(
                                            ps_p[t2][:], buf[:, kc * 128:(kc + 1) * 128],
                                            xn_t[:, kc * TPC + t2 * 512: kc * TPC + (t2 + 1) * 512],
                                            start=(kc == 0), stop=(kc == KCH - 1)),
                                        wait=(ev_cast, ev_xnt, AD.get(("psp_free", t2))),
                                        mark=(kc == KCH - 1))
                            AD[("wbf_free", oc % 2)] = ev_mm[1]
                            ev_ev = [None, None]
                            dst = kv_st if kind == "v" else qk_f
                            for t2 in range(2):
                                ev_ev[t2] = p.add("scalar", lambda s, t2=t2, dst=dst, oc=oc: s.activation(
                                    dst[:, t2 * 512:(t2 + 1) * 512], ps_p[t2][:], AF.Identity,
                                    bias=bias_sb[:, oc:oc + 1]),
                                    wait=(ev_mm[t2], ev_consts,
                                          AD.get(("dst_free", kind))), mark=True)
                                AD[("psp_free", t2)] = ev_ev[t2]
                            if kind == "v":
                                ev_last = p.add("gpsimd", lambda g, oc=oc: g.dma_start(
                                    out=ccv_in.ap()[oc * 128:(oc + 1) * 128, :], in_=kv_st[:]),
                                    wait=(ev_ev[1],), mark=True, dma=True)
                                AD[("dst_free", kind)] = ev_last
                            else:
                                ct = cosq if kind == "q" else cosk
                                st_ = sinq if kind == "q" else sink
                                dst2 = (qt_sb[:, oc * TPC:(oc + 1) * TPC] if kind == "q"
                                        else kv_st[:])
                                evr1 = p.add("vector", lambda v, ct=ct: v.tensor_mul(
                                    rp1[:], qk_f[:], ct[:]),
                                    wait=(ev_ev[0], ev_ev[1], AD.get(("rp_free",))), mark=True)
                                evs0 = p.add("tensor", lambda te: te.matmul(
                                    ps_rope[:, 0:512], rsw[:], qk_f[:, 0:512],
                                    start=True, stop=True),
                                    wait=(ev_ev[0], ev_consts,
                                          AD.get(("psrope_free",))), mark=True)
                                evs1 = p.add("tensor", lambda te: te.matmul(
                                    ps_rope[:, 512:1024], rsw[:], qk_f[:, 512:1024],
                                    start=True, stop=True),
                                    wait=(ev_ev[1],), mark=True)
                                evr23 = p.add("vector", lambda v, st_=st_: v.tensor_mul(
                                    rp2[:], ps_rope[:], st_[:]),
                                    wait=(evs0, evs1, AD.get(("rp2_free",))), mark=True)
                                AD[("dst_free", kind)] = evr23
                                AD[("psrope_free",)] = evr23
                                evr4 = p.add("vector", lambda v, dst2=dst2: v.tensor_add(
                                    dst2, rp1[:], rp2[:]),
                                    wait=(evr1, evr23,
                                          AD.get(("kvst_free",)) if kind == "k" else None),
                                    mark=True)
                                AD[("rp_free",)] = evr4
                                AD[("rp2_free",)] = evr4
                                if kind == "q":
                                    qrope_evs[oc] = evr4
                                if kind == "k":
                                    ev_last = p.add("gpsimd", lambda g, oc=oc: g.dma_start(
                                        out=cck_in.ap()[oc * 128:(oc + 1) * 128, :], in_=kv_st[:]),
                                        wait=(evr4,), mark=True, dma=True)
                                    AD[("kvst_free",)] = ev_last
                        return ev_last

                    qrope_evs = {}
                    ev_kd = proj("k", wk, bk_sb, "k")
                    ev_ccK = p.add("gpsimd", lambda g: g.collective_compute(
                        "AllGather", ALU.bypass, replica_groups=GROUPS,
                        ins=[cck_in.ap()], outs=[cck_out.ap()]),
                        wait=(ev_kd,), mark=True)
                    AD[("dst_free", "v")] = ev_kd
                    ev_vd = proj("v", wv, bv_sb, "v")
                    ev_ccV = p.add("gpsimd", lambda g: g.collective_compute(
                        "AllGather", ALU.bypass, replica_groups=GROUPS,
                        ins=[ccv_in.ap()], outs=[ccv_out.ap()]),
                        wait=(ev_vd,), mark=True)
                    proj("q", wq, bq_sb, "q")

                    wstage_carry = AD.get(("wstage_free",))
                    blkA = sa.enter_context(nc.Block())
                    p.emit(blkA)
                    counts = dict(p.counts)

                with contextlib.ExitStack() as sb:
                    eb = sb.enter_context
                    poff = [0] * 9
                    for qi in range(8):
                        poff[qi + 1] = poff[qi] + (8 - qi) * 512
                    pp = eb(nc.sbuf_tensor("pp", [128, poff[8]], BF16))
                    kt_h = [eb(nc.sbuf_tensor(f"kt{i}", [128, 4096], BF16)) for i in range(2)]
                    vt_h = [eb(nc.sbuf_tensor(f"vt{i}", [128, 4096], BF16)) for i in range(2)]
                    vtok = eb(nc.sbuf_tensor("vtok", [128, 4096], BF16))
                    ptb = [eb(nc.sbuf_tensor(f"ptb{i}", [128, 1024], BF16)) for i in range(2)]
                    nm_sb = [eb(nc.sbuf_tensor(f"nm{i}", [128, 64], F32)) for i in range(2)]
                    sc_sb = [eb(nc.sbuf_tensor(f"sc{i}", [128, 64], F32)) for i in range(2)]
                    f_sb = eb(nc.sbuf_tensor("f_sb", [128, 64], F32))
                    g_sb = eb(nc.sbuf_tensor("g_sb", [128, 64], F32))
                    sf_sb = eb(nc.sbuf_tensor("sf_sb", [128, 64], F32))
                    nmg_sb = eb(nc.sbuf_tensor("nmg_sb", [128, 8], F32))
                    l_sb = eb(nc.sbuf_tensor("l_sb", [128, 8], F32))
                    r_sb = eb(nc.sbuf_tensor("r_sb", [128, 8], F32))
                    ps_s = [eb(nc.psum_tensor(f"ps_s{i}", [128, 512], F32)) for i in range(3)]
                    ps_pt = eb(nc.psum_tensor("ps_pt", [128, 1024], BF16))
                    ps_o = [eb(nc.psum_tensor(f"ps_o{i}", [128, 512], F32)) for i in range(2)]
                    ps_vt = eb(nc.psum_tensor("ps_vt", [128, 1024], BF16))

                    p = Prog(sems, counts, barrier=True)
                    AD = {}
                    sslot = [0]

                    for h in range(NH):
                        kb_ = kt_h[h % 2]
                        vb_ = vt_h[h % 2]
                        ev_kl = ev_vl = None
                        for c in range(8):
                            for r in range(4):
                                ev_kl = p.add("sync", lambda s, c=c, r=r, kb_=kb_, h=h: s.dma_start(
                                    out=kb_[:, (c * 4 + r) * 128:(c * 4 + r + 1) * 128],
                                    in_=cck_out.ap()[2048 * r + 128 * h: 2048 * r + 128 * (h + 1),
                                                     128 * (7 - c):128 * (8 - c)]),
                                    wait=(ev_ccK, AD.get(("kbuf_free", h % 2)),),
                                    mark=True, dma=True)
                        for c in range(8):
                            for r in range(4):
                                ev_vl = p.add("sync", lambda s, c=c, r=r, vb_=vb_, h=h: s.dma_start(
                                    out=vb_[:, (c * 4 + r) * 128:(c * 4 + r + 1) * 128],
                                    in_=ccv_out.ap()[2048 * r + 128 * h: 2048 * r + 128 * (h + 1),
                                                     128 * (7 - c):128 * (8 - c)]),
                                    wait=(ev_ccV, AD.get(("vbuf_free", h % 2)),),
                                    mark=True, dma=True)

                        ev_vtok = None
                        for c8 in range(4):
                            ev_tr = None
                            for kk in range(8):
                                kbi = c8 * 8 + kk
                                ev_tr = p.add("tensor", lambda te, kbi=kbi, kk=kk, vb_=vb_: te.transpose(
                                    ps_vt[:, kk * 128:(kk + 1) * 128],
                                    vb_[:, kbi * 128:(kbi + 1) * 128], ident[:]),
                                    wait=(ev_vl, AD.get(("psvt_free",)),
                                          AD.get(("vtok_free",))),
                                    mark=(kk == 7))
                            ev_vtok = p.add("vector", lambda v, c8=c8: v.tensor_copy(
                                vtok[:, c8 * 1024:(c8 + 1) * 1024], ps_vt[:]),
                                wait=(ev_tr,), mark=True)
                            AD[("psvt_free",)] = ev_vtok
                        AD[("vbuf_free", h % 2)] = Ev("tensor", p.counts["tensor"])

                        nm = nm_sb[h % 2]
                        sc = sc_sb[h % 2]
                        ev_exp = {}
                        ev_qk_last = None
                        ev_pscale = {}
                        for c in range(8):
                            for qi in range(8 - c):
                                sl = sslot[0] % 3
                                sslot[0] += 1
                                psb = ps_s[sl]
                                ev_mm = p.add("tensor", lambda te, h=h, qi=qi, c=c, psb=psb, kb_=kb_: te.matmul(
                                    psb[:], qt_sb[:, h * TPC + qi * 128: h * TPC + (qi + 1) * 128],
                                    kb_[:, c * 512:(c + 1) * 512], start=True, stop=True),
                                    wait=(ev_kl, AD.get(("ps_s_free", sl)),
                                          qrope_evs[h] if (qi == 0 and c == 0) else None),
                                    mark=True)
                                ev_qk_last = ev_mm
                                if qi == 7 - c:
                                    ev_mm = p.add("vector", lambda v, psb=psb: v.tensor_add(
                                        psb[:], psb[:], mask[:]),
                                        wait=(ev_mm, ev_consts), mark=True)
                                ev_mx = p.add("vector", lambda v, psb=psb, qi=qi, c=c, nm=nm: v.reduce_max(
                                    nm[:, qi * 8 + c: qi * 8 + c + 1], psb[:], axis=AX.X,
                                    negate=True),
                                    wait=(ev_mm, AD.get(("nm_free", h % 2, qi)),), mark=True)
                                ev_e = p.add("scalar", lambda s, psb=psb, qi=qi, c=c, nm=nm, sc=sc: s.activation(
                                    pp[:, poff[qi] + c * 512: poff[qi] + (c + 1) * 512],
                                    psb[:], AF.Exp, bias=nm[:, qi * 8 + c: qi * 8 + c + 1],
                                    accum_out=sc[:, qi * 8 + c: qi * 8 + c + 1]),
                                    wait=(ev_mx, AD.get(("pp_free", h, qi, c)),
                                          AD.get(("sc_free", h % 2, qi))), mark=True)
                                ev_exp[(qi, c)] = ev_e
                                AD[("ps_s_free", sl)] = ev_e

                                if qi == 7 - c:
                                    cnt = 8 - qi
                                    evc1 = p.add("vector", lambda v, qi=qi, cnt=cnt, nm=nm: v.tensor_reduce(
                                        nmg_sb[:, qi:qi + 1], nm[:, qi * 8: qi * 8 + cnt],
                                        axis=AX.X, op=ALU.min),
                                        wait=(ev_mx, AD.get(("nmg_free", qi))), mark=True)
                                    evc2 = p.add("scalar", lambda s, qi=qi, cnt=cnt, nm=nm: s.activation(
                                        f_sb[:, qi * 8: qi * 8 + cnt],
                                        nm[:, qi * 8: qi * 8 + cnt], AF.Exp,
                                        bias=nmg_sb[:, qi:qi + 1], scale=-1.0),
                                        wait=(evc1, AD.get(("f_free", qi))), mark=True)
                                    evc3 = p.add("vector", lambda v, qi=qi, cnt=cnt, sc=sc: v.tensor_mul(
                                        sf_sb[:, qi * 8: qi * 8 + cnt],
                                        sc[:, qi * 8: qi * 8 + cnt],
                                        f_sb[:, qi * 8: qi * 8 + cnt]),
                                        wait=(evc2, ev_e), mark=True)
                                    evc4 = p.add("vector", lambda v, qi=qi, cnt=cnt: v.reduce_sum(
                                        l_sb[:, qi:qi + 1], sf_sb[:, qi * 8: qi * 8 + cnt],
                                        axis=AX.X), wait=(evc3,), mark=True)
                                    evc5 = p.add("vector", lambda v, qi=qi: v.reciprocal(
                                        r_sb[:, qi:qi + 1], l_sb[:, qi:qi + 1]),
                                        wait=(evc4,), mark=True)
                                    evc6 = p.add("vector", lambda v, qi=qi, cnt=cnt: v.tensor_scalar(
                                        g_sb[:, qi * 8: qi * 8 + cnt],
                                        f_sb[:, qi * 8: qi * 8 + cnt],
                                        r_sb[:, qi:qi + 1], None, ALU.mult),
                                        wait=(evc5,), mark=True)
                                    AD[("nm_free", h % 2, qi)] = evc2
                                    AD[("sc_free", h % 2, qi)] = evc3
                                    AD[("nmg_free", qi)] = evc2
                                    AD[("f_free", qi)] = evc6
                                    evs = None
                                    for c2 in range(cnt):
                                        evs = p.add("vector", lambda v, qi=qi, c2=c2: v.tensor_scalar(
                                            pp[:, poff[qi] + c2 * 512: poff[qi] + (c2 + 1) * 512],
                                            pp[:, poff[qi] + c2 * 512: poff[qi] + (c2 + 1) * 512],
                                            g_sb[:, qi * 8 + c2: qi * 8 + c2 + 1], None, ALU.mult),
                                            wait=(evc6, ev_exp[(qi, c2)]),
                                            mark=(c2 == cnt - 1))
                                    ev_pscale[qi] = evs

                        ev_pv_last = None
                        ev_pt_last = None
                        for c in range(8):
                            pref = 8 - c
                            for kbi in range(4):
                                pb = ptb[(c * 4 + kbi) % 2]
                                ev_tr = None
                                for qi in range(pref):
                                    ev_tr = p.add("tensor", lambda te, qi=qi, c=c, kbi=kbi: te.transpose(
                                        ps_pt[:, qi * 128:(qi + 1) * 128],
                                        pp[:, poff[qi] + c * 512 + kbi * 128:
                                           poff[qi] + c * 512 + (kbi + 1) * 128], ident[:]),
                                        wait=(ev_pscale[qi], AD.get(("pspt_free",))),
                                        mark=(qi == pref - 1))
                                ev_ptev = p.add("vector", lambda v, pref=pref, pb=pb: v.tensor_copy(
                                    pb[:, 0:pref * 128], ps_pt[:, 0:pref * 128]),
                                    wait=(ev_tr, AD.get(("ptb_free", (c * 4 + kbi) % 2))),
                                    mark=True)
                                AD[("pspt_free",)] = ev_ptev
                                ev_pt_last = ev_ptev
                                kbg = c * 4 + kbi
                                n0 = min(512, pref * 128)
                                ev_pv = p.add("tensor", lambda te, kbg=kbg, n0=n0, pb=pb, c=c: te.matmul(
                                    ps_o[0][:, 0:n0], vtok[:, kbg * 128:(kbg + 1) * 128],
                                    pb[:, 0:n0], start=(c == 0 and kbg == 0),
                                    stop=(c == 7 and kbg == 31)),
                                    wait=(ev_vtok, ev_ptev, AD.get(("ps_o_free", 0))), mark=True)
                                if pref > 4:
                                    n1 = (pref - 4) * 128
                                    ev_pv = p.add("tensor", lambda te, kbg=kbg, n1=n1, pb=pb, c=c: te.matmul(
                                        ps_o[1][:, 0:n1], vtok[:, kbg * 128:(kbg + 1) * 128],
                                        pb[:, 512:512 + n1], start=(c == 0 and kbg == 0),
                                        stop=(c == 3 and kbg == 15)),
                                        wait=(AD.get(("ps_o_free", 1)),), mark=True)
                                ev_pv_last = ev_pv
                                AD[("ptb_free", (c * 4 + kbi) % 2)] = ev_pv
                        AD[("pp_free_src",)] = ev_pt_last
                        for qi in range(8):
                            for c in range(8 - qi):
                                AD[("pp_free", h + 1, qi, c)] = ev_pt_last
                        ev_ao0 = p.add("vector", lambda v, h=h: v.tensor_copy(
                            a_t[:, h * TPC: h * TPC + 512], ps_o[0][:]),
                            wait=(ev_pv_last,), mark=True)
                        ev_ao1 = p.add("vector", lambda v, h=h: v.tensor_copy(
                            a_t[:, h * TPC + 512: h * TPC + 1024], ps_o[1][:]),
                            wait=(ev_pv_last,), mark=True)
                        AD[("ps_o_free", 0)] = ev_ao0
                        AD[("ps_o_free", 1)] = ev_ao1
                        AD[("kbuf_free", h % 2)] = ev_qk_last
                        AD[("vtok_free",)] = ev_pv_last

                    blkB = sb.enter_context(nc.Block())
                    p.emit(blkB)
                    counts = dict(p.counts)

            with contextlib.ExitStack() as s2:
                e2 = s2.enter_context
                wom = e2(nc.sbuf_tensor("wom", [128, KCH * 2048], BF16))
                xstr = [e2(nc.sbuf_tensor(f"xstr{i}", [128, H], F32)) for i in range(2)]
                xa = [e2(nc.sbuf_tensor(f"xa{i}", [128, H], F32)) for i in range(2)]
                xat = [e2(nc.sbuf_tensor(f"xat{i}", [128, H], F32)) for i in range(2)]
                ps_y = [e2(nc.psum_tensor(f"ps_y{i}", [128, 512], F32)) for i in range(4)]
                ps_t2 = e2(nc.psum_tensor("ps_t2", [128, 512], F32))

                p = Prog(sems, counts, barrier=True)
                AD = {}
                AD[("wstage_free",)] = wstage_carry
                x2_ev = {}
                x2t_ev = {}
                ev_cast = {}
                for fc in range(KCH):
                    ev_w = p.add("sync", lambda s, fc=fc: s.dma_start(
                        out=wstage[:], in_=wo[fc]),
                        wait=(AD.get(("wstage_free",)),), mark=True, dma=True)
                    ev_cast[fc] = p.add("vector", lambda v, fc=fc: v.tensor_copy(
                        wom[:, fc * 2048:(fc + 1) * 2048], wstage[:]),
                        wait=(ev_w,), mark=True)
                    AD[("wstage_free",)] = ev_cast[fc]
                for tq in range(8):
                    xb = xstr[tq % 2]
                    ev_x = p.add("gpsimd", lambda g, tq=tq, xb=xb: g.dma_start(
                        out=xb[:], in_=x_in[tq * 128:(tq + 1) * 128, :]),
                        wait=(AD.get(("xstr_free", tq % 2)),), mark=True, dma=True)
                    ev_mm = [None] * 4
                    for fc in range(KCH):
                        for o4 in range(4):
                            ev_mm[o4] = p.add("tensor", lambda te, tq=tq, fc=fc, o4=o4: te.matmul(
                                ps_y[o4][:], a_t[:, fc * TPC + tq * 128: fc * TPC + (tq + 1) * 128],
                                wom[:, fc * 2048 + o4 * 512: fc * 2048 + (o4 + 1) * 512],
                                start=(fc == 0), stop=(fc == KCH - 1)),
                                wait=(ev_cast[fc], AD.get(("ps_y_free", o4))), mark=(fc == KCH - 1))
                    xab = xa[tq % 2]
                    ev_add = [None] * 4
                    for o4 in range(4):
                        ev_add[o4] = p.add("vector", lambda v, o4=o4, xab=xab, xb=xb: v.tensor_add(
                            xab[:, o4 * 512:(o4 + 1) * 512], ps_y[o4][:],
                            xb[:, o4 * 512:(o4 + 1) * 512]),
                            wait=(ev_mm[o4], ev_x, AD.get(("xa_free", tq % 2))), mark=True)
                        AD[("ps_y_free", o4)] = ev_add[o4]
                    AD[("xstr_free", tq % 2)] = ev_add[3]
                    ev_bo = p.add("vector", lambda v, xab=xab: v.tensor_add(
                        xab[:], xab[:], bo_sb[:]),
                        wait=(ev_add[0], ev_add[1], ev_add[2], ev_add[3]), mark=True)
                    ev_x2 = p.add("gpsimd", lambda g, tq=tq, xab=xab: g.dma_start(
                        out=x2_dram.ap()[tq * 128:(tq + 1) * 128, :], in_=xab[:]),
                        wait=(ev_bo,), mark=True, dma=True)
                    xatb = xat[tq % 2]
                    ev_x2t = None
                    for q4 in range(4):
                        ev_tr = None
                        for kk in range(4):
                            fc = q4 * 4 + kk
                            ev_tr = p.add("tensor", lambda te, fc=fc, kk=kk, xab=xab: te.transpose(
                                ps_t2[:, kk * 128:(kk + 1) * 128],
                                xab[:, fc * 128:(fc + 1) * 128],
                                identf[:]),
                                wait=(ev_bo, AD.get(("ps_t2_free",))), mark=(kk == 3))
                        ev_x2t = p.add("vector", lambda v, q4=q4, xatb=xatb: v.tensor_copy(
                            xatb[:, q4 * 512:(q4 + 1) * 512], ps_t2[:]),
                            wait=(ev_tr, AD.get(("xat_free", tq % 2))), mark=True)
                        AD[("ps_t2_free",)] = ev_x2t
                    ev_x2td = None
                    for fc in range(KCH):
                        ev_x2td = p.add("gpsimd", lambda g, tq=tq, fc=fc, xatb=xatb: g.dma_start(
                            out=x2t_dram.ap()[fc * 128:(fc + 1) * 128, tq * 128:(tq + 1) * 128],
                            in_=xatb[:, fc * 128:(fc + 1) * 128]),
                            wait=(ev_x2t,), mark=True, dma=True)
                    AD[("xa_free", tq % 2)] = ev_x2td
                    AD[("xat_free", tq % 2)] = ev_x2td
                    x2_ev[tq] = ev_x2
                    x2t_ev[tq] = ev_x2td

                wstage_carry = AD.get(("wstage_free",))
                blkB2 = s2.enter_context(nc.Block())
                p.emit(blkB2)
                counts = dict(p.counts)

        with contextlib.ExitStack() as sc:
            ec = sc.enter_context
            h_t = ec(nc.sbuf_tensor("h_t", [128, ICH * TPC], BF16))
            xstr2 = [ec(nc.sbuf_tensor(f"xstr2_{i}", [128, H], F32)) for i in range(2)]
            ps_g = ec(nc.psum_tensor("ps_g", [128, 1024], F32))
            ps_u = ec(nc.psum_tensor("ps_u", [128, 1024], F32))
            ps_tr3 = ec(nc.psum_tensor("ps_tr3", [128, 1024], BF16))
            ga_f = ec(nc.sbuf_tensor("ga_f", [128, 1024], F32))

            p = Prog(sems, counts, barrier=True)
            AD = {}
            AD[("wstage_free",)] = wstage_carry

            with contextlib.ExitStack() as scz:
                z_t = scz.enter_context(nc.sbuf_tensor("z_t", [128, KCH * TPC], BF16))
                zn_tok = scz.enter_context(nc.sbuf_tensor("zn_tok", [128, H], BF16))
                sq2 = scz.enter_context(nc.sbuf_tensor("sq2", [128, H], F32))

                tr2_last = {}
                for t in range(8):
                    xb = xstr2[t % 2]
                    ev_x = p.add("gpsimd", lambda g, t=t, xb=xb: g.dma_start(
                        out=xb[:], in_=x2_dram.ap()[t * 128:(t + 1) * 128, :]),
                        wait=(AD.get(("x2_free", t % 2)), x2_ev[t]), mark=True, dma=True)
                    ev_sum = p.add("vector", lambda v, t=t, xb=xb: v.reduce_sum(
                        st1[:, t:t + 1], xb[:], axis=AX.X), wait=(ev_x,), mark=True)
                    ev_sq = p.add("scalar", lambda s, t=t, xb=xb: s.activation(
                        sq2[:], xb[:], AF.Square, accum_out=st2[:, t:t + 1]),
                        wait=(ev_x,), mark=True)
                    ev1 = p.add("vector", lambda v, t=t: v.tensor_scalar(
                        st1[:, t:t + 1], st1[:, t:t + 1], 1.0 / H, None, ALU.mult),
                        wait=(ev_sum,), mark=True)
                    ev2 = p.add("vector", lambda v, t=t: v.tensor_scalar(
                        st2[:, t:t + 1], st2[:, t:t + 1], 1.0 / H, None, ALU.mult),
                        wait=(ev_sq,), mark=True)
                    ev3 = p.add("vector", lambda v, t=t: v.tensor_mul(
                        st3[:, t:t + 1], st1[:, t:t + 1], st1[:, t:t + 1]),
                        wait=(ev1,), mark=True)
                    ev4 = p.add("vector", lambda v, t=t: v.tensor_sub(
                        st2[:, t:t + 1], st2[:, t:t + 1], st3[:, t:t + 1]),
                        wait=(ev2, ev3), mark=True)
                    ev5 = p.add("vector", lambda v, t=t: v.tensor_scalar(
                        st2[:, t:t + 1], st2[:, t:t + 1], EPS, None, ALU.add),
                        wait=(ev4,), mark=True)
                    ev6 = p.add("vector", lambda v, t=t: v.reciprocal(
                        st3[:, t:t + 1], st2[:, t:t + 1]), wait=(ev5,), mark=True)
                    ev7 = p.add("scalar", lambda s, t=t: s.activation(
                        st2[:, t:t + 1], st3[:, t:t + 1], AF.Sqrt),
                        wait=(ev6,), mark=True)
                    ev8 = p.add("vector", lambda v, t=t: v.tensor_mul(
                        st3[:, t:t + 1], st1[:, t:t + 1], st2[:, t:t + 1]),
                        wait=(ev7,), mark=True)
                    ev9 = p.add("vector", lambda v, t=t: v.tensor_scalar(
                        st3[:, t:t + 1], st3[:, t:t + 1], -1.0, None, ALU.mult),
                        wait=(ev8,), mark=True)
                    ev_norm = p.add("scalar", lambda s, t=t, xb=xb: s.activation(
                        zn_tok[:], xb[:], AF.Identity,
                        bias=st3[:, t:t + 1], scale=st2[:, t:t + 1]),
                        wait=(ev9, tr2_last.get(t - 1)), mark=True)
                    AD[("x2_free", t % 2)] = ev_norm
                    for half in range(2):
                        ev_tr = None
                        for kk in range(8):
                            kc = half * 8 + kk
                            ev_tr = p.add("tensor", lambda te, kc=kc, kk=kk: te.transpose(
                                ps_tr3[:, kk * 128:(kk + 1) * 128],
                                zn_tok[:, kc * 128:(kc + 1) * 128], ident[:]),
                                wait=(ev_norm, AD.get(("pstr3_free",))), mark=(kk == 7))
                        ev_ev = p.add("vector", lambda v, t=t, half=half: v.tensor_copy(
                            z_t[:].rearrange("q (kc tt) -> q kc tt", kc=KCH)[
                                :, half * 8:(half + 1) * 8, t * 128:(t + 1) * 128],
                            ps_tr3[:].rearrange("q (kk r) -> q kk r", kk=8)),
                            wait=(ev_tr,), mark=True)
                        AD[("pstr3_free",)] = ev_ev
                        if half == 1:
                            tr2_last[t] = ev_tr
                        if t == 7 and half == 1:
                            ev_znt = ev_ev

                for ic in range(ICH):
                    gb = wbf[ic % 2]
                    ub = wbf2[ic % 2]
                    ev_w1 = p.add("sync", lambda s, ic=ic: s.dma_start(
                        out=wstage[:], in_=wg[ic]),
                        wait=(AD.get(("wstage_free",)),), mark=True, dma=True)
                    ev_c1 = p.add("vector", lambda v, gb=gb: v.tensor_copy(
                        gb[:], wstage[:]),
                        wait=(ev_w1, AD.get(("gb_free", ic % 2))), mark=True)
                    ev_w2 = p.add("sync", lambda s, ic=ic: s.dma_start(
                        out=wstage[:], in_=wu[ic]), wait=(ev_c1,), mark=True, dma=True)
                    ev_c2 = p.add("vector", lambda v, ub=ub: v.tensor_copy(
                        ub[:], wstage[:]),
                        wait=(ev_w2, AD.get(("ub_free", ic % 2))), mark=True)
                    AD[("wstage_free",)] = ev_c2
                    ev_gmm = [None, None]
                    ev_umm = [None, None]
                    for t2 in range(2):
                        for kc in range(KCH):
                            ev_gmm[t2] = p.add("tensor", lambda te, gb=gb, kc=kc, t2=t2: te.matmul(
                                ps_g[:, t2 * 512:(t2 + 1) * 512],
                                gb[:, kc * 128:(kc + 1) * 128],
                                z_t[:, kc * TPC + t2 * 512: kc * TPC + (t2 + 1) * 512],
                                start=(kc == 0), stop=(kc == KCH - 1)),
                                wait=(ev_c1, ev_znt, AD.get(("ps_g_free",))),
                                mark=(kc == KCH - 1))
                    for t2 in range(2):
                        for kc in range(KCH):
                            ev_umm[t2] = p.add("tensor", lambda te, ub=ub, kc=kc, t2=t2: te.matmul(
                                ps_u[:, t2 * 512:(t2 + 1) * 512],
                                ub[:, kc * 128:(kc + 1) * 128],
                                z_t[:, kc * TPC + t2 * 512: kc * TPC + (t2 + 1) * 512],
                                start=(kc == 0), stop=(kc == KCH - 1)),
                                wait=(ev_c2, AD.get(("ps_u_free",))),
                                mark=(kc == KCH - 1))
                    AD[("gb_free", ic % 2)] = ev_gmm[1]
                    AD[("ub_free", ic % 2)] = ev_umm[1]
                    ev_silu = p.add("scalar", lambda s, ic=ic: s.activation(
                        ga_f[:], ps_g[:], AF.Silu, scale=gs_sb[:, ic:ic + 1]),
                        wait=(ev_gmm[0], ev_gmm[1], AD.get(("ga_free",))), mark=True)
                    AD[("ps_g_free",)] = ev_silu
                    ev_h = p.add("vector", lambda v, ic=ic: v.scalar_tensor_tensor(
                        h_t[:, ic * TPC:(ic + 1) * TPC], ps_u[:], us_sb[:, ic:ic + 1],
                        ga_f[:], ALU.mult, ALU.mult),
                        wait=(ev_umm[0], ev_umm[1], ev_silu), mark=True)
                    AD[("ps_u_free",)] = ev_h
                    AD[("ga_free",)] = ev_h
                    AD[("h_done", ic)] = ev_h
                last_zt_read = ev_umm[1]
                for i_ in range(2):
                    AD[("db_free", i_)] = last_zt_read
                    AD[("x2t_free", i_)] = last_zt_read
                    AD[("yt_free", i_)] = last_zt_read

            with contextlib.ExitStack() as scd:
                dbf = [scd.enter_context(nc.sbuf_tensor(f"dbf{i}", [128, ICH * 128], BF16))
                       for i in range(2)]
                x2t_sb = [scd.enter_context(nc.sbuf_tensor(f"x2t{i}", [128, TPC], F32))
                          for i in range(2)]
                yt_sb = [scd.enter_context(nc.sbuf_tensor(f"yt{i}", [128, TPC], F32))
                         for i in range(2)]
                ps_y2 = scd.enter_context(nc.psum_tensor("ps_dn", [128, 1024], F32))
                ev_ydma = None
                for oc in range(OCH):
                    db = dbf[oc % 2]
                    ev_cast = None
                    for piece in range(3):
                        lo = piece * 2048
                        hi = min(ICH * 128, lo + 2048)
                        ev_w = p.add("sync", lambda s, oc=oc, lo=lo, hi=hi: s.dma_start(
                            out=wstage[:, 0:hi - lo], in_=wd[oc][:, lo:hi]),
                            wait=(AD.get(("wstage_free",)),), mark=True, dma=True)
                        ev_cast = p.add("vector", lambda v, db=db, lo=lo, hi=hi: v.tensor_copy(
                            db[:, lo:hi], wstage[:, 0:hi - lo]),
                            wait=(ev_w, AD.get(("db_free", oc % 2))), mark=True)
                        AD[("wstage_free",)] = ev_cast
                    xtb = x2t_sb[oc % 2]
                    ev_xt = p.add("gpsimd", lambda g, oc=oc, xtb=xtb: g.dma_start(
                        out=xtb[:], in_=x2t_dram.ap()[oc * 128:(oc + 1) * 128, :]),
                        wait=(AD.get(("x2t_free", oc % 2)), x2t_ev[7]), mark=True, dma=True)
                    ev_mm = [None, None]
                    for ic in range(ICH):
                        for t2 in range(2):
                            ev_mm[t2] = p.add("tensor", lambda te, db=db, ic=ic, t2=t2: te.matmul(
                                ps_y2[:, t2 * 512:(t2 + 1) * 512],
                                db[:, ic * 128:(ic + 1) * 128],
                                h_t[:, ic * TPC + t2 * 512: ic * TPC + (t2 + 1) * 512],
                                start=(ic == 0), stop=(ic == ICH - 1)),
                                wait=(ev_cast, AD[("h_done", ic)],
                                      AD.get(("ps_y2_free",))),
                                mark=(ic == ICH - 1))
                    AD[("db_free", oc % 2)] = ev_mm[1]
                    ytb = yt_sb[oc % 2]
                    ev_y = p.add("vector", lambda v, oc=oc, ytb=ytb, xtb=xtb: v.scalar_tensor_tensor(
                        ytb[:], ps_y2[:], ds_sb[:, oc:oc + 1], xtb[:],
                        ALU.mult, ALU.add),
                        wait=(ev_mm[0], ev_mm[1], ev_xt, AD.get(("yt_free", oc % 2))),
                        mark=True)
                    AD[("ps_y2_free",)] = ev_y
                    AD[("x2t_free", oc % 2)] = ev_y
                    ev_ydma = p.add("gpsimd", lambda g, oc=oc, ytb=ytb: g.dma_start(
                        out=y_out[oc * 128:(oc + 1) * 128, :], in_=ytb[:]),
                        wait=(ev_y,), mark=True, dma=True)
                    AD[("yt_free", oc % 2)] = ev_ydma
                p.add("gpsimd", lambda g, v=ev_ydma.val: g.wait_ge(sems["gpsimd"], v))

                blkC = sc.enter_context(nc.Block())
                p.emit(blkC)

    return nc


_NC_CACHE = [None]


def _rope_tables(pos, scale):
    half = D // 2
    inv_freq = 1.0 / (10000.0 ** (np.arange(half, dtype=np.float32) / half))
    ang = pos[None, :].astype(np.float32) * inv_freq[:, None]
    cos = np.cos(ang).astype(np.float32)
    sin = np.sin(ang).astype(np.float32)
    cosT = np.concatenate([cos, cos], axis=0) * scale
    sinT = np.concatenate([-sin, sin], axis=0) * scale
    return (cosT.astype(ml_dtypes.bfloat16), sinT.astype(ml_dtypes.bfloat16))


def kernel(L0_attn_x, L0_attn_Wq, L0_attn_bq, L0_attn_Wk, L0_attn_bk,
           L0_attn_Wv, L0_attn_bv, L0_attn_Wo, L0_attn_bo,
           L0_ffn_gate_w, L0_ffn_up_w, L0_ffn_down_w,
           L0_ffn_gate_s, L0_ffn_up_s, L0_ffn_down_s):
    x = np.ascontiguousarray(np.asarray(L0_attn_x, dtype=np.float32))
    Wq = np.asarray(L0_attn_Wq, dtype=np.float32)
    Wk = np.asarray(L0_attn_Wk, dtype=np.float32)
    Wv = np.asarray(L0_attn_Wv, dtype=np.float32)
    Wo = np.asarray(L0_attn_Wo, dtype=np.float32)
    bq = np.asarray(L0_attn_bq, dtype=np.float32)
    bk = np.asarray(L0_attn_bk, dtype=np.float32)
    bv = np.asarray(L0_attn_bv, dtype=np.float32)
    bo = np.asarray(L0_attn_bo, dtype=np.float32)
    gw = np.asarray(L0_ffn_gate_w, dtype=np.float32)
    uw = np.asarray(L0_ffn_up_w, dtype=np.float32)
    dw = np.asarray(L0_ffn_down_w, dtype=np.float32)
    gs = np.asarray(L0_ffn_gate_s, dtype=np.float32)
    us = np.asarray(L0_ffn_up_s, dtype=np.float32)
    ds = np.asarray(L0_ffn_down_s, dtype=np.float32)

    perm = np.concatenate([
        np.concatenate([h * D + 2 * np.arange(64), h * D + 2 * np.arange(64) + 1])
        for h in range(NH)]).astype(np.int64)

    def pack_lhsT(w_rows_out_cols_in, no, nk):
        a = w_rows_out_cols_in.reshape(no, 128, nk, 128)
        return np.ascontiguousarray(a.transpose(0, 3, 2, 1)).reshape(no, 128, nk * 128)

    wq_p = pack_lhsT(Wq[perm], OCH, KCH)
    wk_p = pack_lhsT(Wk[perm], OCH, KCH)
    wv_p = pack_lhsT(Wv, OCH, KCH)
    wg_p = pack_lhsT(gw, ICH, KCH)
    wu_p = pack_lhsT(uw, ICH, KCH)
    wd_p = pack_lhsT(dw, OCH, ICH)
    wo_p = np.ascontiguousarray(Wo.T.reshape(KCH, 128, H))

    bq_p = np.ascontiguousarray(bq[perm].reshape(OCH, 128).T)
    bk_p = np.ascontiguousarray(bk[perm].reshape(OCH, 128).T)
    bv_p = np.ascontiguousarray(bv.reshape(OCH, 128).T)
    gs_p = np.ascontiguousarray(gs.reshape(ICH, 128).T)
    us_p = np.ascontiguousarray(us.reshape(ICH, 128).T)
    ds_p = np.ascontiguousarray(ds.reshape(OCH, 128).T)
    bo_bcast = np.ascontiguousarray(np.broadcast_to(bo, (128, H)))
    ident = np.eye(128, dtype=ml_dtypes.bfloat16)
    identf = np.eye(128, dtype=np.float32)
    rswap = np.zeros((128, 128), dtype=ml_dtypes.bfloat16)
    for i in range(128):
        rswap[i, (i + 64) % 128] = 1

    if _NC_CACHE[0] is None:
        _NC_CACHE[0] = build_nc()
    nc = _NC_CACHE[0]

    in_maps = []
    shared = dict(wq=wq_p, wk=wk_p, wv=wv_p, wo=wo_p, wg=wg_p, wu=wu_p, wd=wd_p,
                  bq=bq_p, bk=bk_p, bv=bv_p, bo_bcast=bo_bcast,
                  gs=gs_p, us=us_p, ds=ds_p, ident=ident, identf=identf,
                  rswap=rswap)
    for core in range(8):
        b = core // 4
        j = core % 4
        gtiles = [4 * (7 - qi) + j for qi in range(8)]
        rows = np.concatenate([np.arange(g * 128, (g + 1) * 128) for g in gtiles])
        xc = np.ascontiguousarray(x[b].reshape(S, H)[rows])
        pos = rows.astype(np.float32)
        cq, sq = _rope_tables(pos, 1.0 / np.sqrt(np.float32(D)))
        ck, sk = _rope_tables(pos, 1.0)
        m = np.zeros((128, 512), dtype=np.float32)
        for pp_ in range(128):
            m[pp_, j * 128 + pp_ + 1:] = NEGM
        im = dict(shared)
        im.update(x=xc, cosq=cq, sinq=sq, cosk=ck, sink=sk,
                  mask=np.ascontiguousarray(m))
        in_maps.append(im)

    res = run_bass_kernel_spmd(nc, in_maps, core_ids=list(range(8)))

    out = np.empty((B, S, H), dtype=np.float32)
    for core in range(8):
        b = core // 4
        j = core % 4
        yc = res.results[core]["y"]
        for qi in range(8):
            g = 4 * (7 - qi) + j
            out[b, g * 128:(g + 1) * 128, :] = yc[:, qi * 128:(qi + 1) * 128].T
    return out


# version 05 (final)
# speedup vs baseline: 1.0091x; 1.0091x over previous
import contextlib
import numpy as np
import ml_dtypes
import concourse.bass as bass
import concourse.mybir as mybir
from concourse.bass_utils import run_bass_kernel_spmd

F32 = mybir.dt.float32
BF16 = mybir.dt.bfloat16
AF = mybir.ActivationFunctionType
ALU = mybir.AluOpType
AX = mybir.AxisListType

B, S, H, NH, D, INTER = 2, 4096, 2048, 16, 128, 5632
TPC = 1024
KCH = H // 128
OCH = H // 128
ICH = INTER // 128
EPS = 1e-5
NEGM = -30000.0
GROUPS = [[0, 1, 2, 3], [4, 5, 6, 7]]


class Ev:
    __slots__ = ("eng", "val")

    def __init__(self, eng, val):
        self.eng = eng
        self.val = val


class Prog:
    ENGINES = ("tensor", "vector", "scalar", "gpsimd", "sync")

    def __init__(self, sems, base, barrier=False):
        self.sems = sems
        self.ops = {e: [] for e in self.ENGINES}
        self.counts = dict(base)
        if barrier is True:
            engs = self.ENGINES
        elif barrier:
            engs = barrier
        else:
            engs = ()
        self.barrier_evs = tuple(
            Ev(e, base[e]) for e in engs if base[e] > 0)

    def add(self, eng, fn, wait=(), mark=False, dma=False):
        waits = tuple(w for w in wait if w is not None)
        if not self.ops[eng] and self.barrier_evs:
            waits += tuple(w for w in self.barrier_evs if w.eng != eng)
        ev = None
        if mark:
            self.counts[eng] += 16 if dma else 1
            ev = Ev(eng, self.counts[eng])
        self.ops[eng].append((fn, waits, ev, dma))
        return ev

    def emit(self, block):
        for eng_name in self.ENGINES:
            ops = self.ops[eng_name]
            if not ops:
                continue

            def body(engine, ops=ops, eng_name=eng_name):
                last_wait = {}
                n_since_drain = 0
                for fn, waits, ev, dma in ops:
                    need_drain = False
                    for w in waits:
                        if w.eng == eng_name:
                            need_drain = True
                            continue
                        if last_wait.get(w.eng, -1) >= w.val:
                            continue
                        last_wait[w.eng] = w.val
                        engine.wait_ge(self.sems[w.eng], w.val)
                    if need_drain and n_since_drain > 0:
                        engine.drain()
                        n_since_drain = 0
                    instr = fn(engine)
                    n_since_drain += 1
                    if ev is not None:
                        instr.then_inc(self.sems[eng_name], 16 if dma else 1)

            getattr(block, eng_name)(body)


def build_nc():
    nc = bass.Bass("TRN2", target_bir_lowering=False, detect_race_conditions=False)

    x_in = nc.dram_tensor("x", [TPC, H], F32, kind="ExternalInput").ap()
    wq = nc.dram_tensor("wq", [OCH, 128, KCH * 128], F32, kind="ExternalInput").ap()
    wk = nc.dram_tensor("wk", [OCH, 128, KCH * 128], F32, kind="ExternalInput").ap()
    wv = nc.dram_tensor("wv", [OCH, 128, KCH * 128], F32, kind="ExternalInput").ap()
    wo = nc.dram_tensor("wo", [KCH, 128, H], F32, kind="ExternalInput").ap()
    wg = nc.dram_tensor("wg", [ICH, 128, KCH * 128], F32, kind="ExternalInput").ap()
    wu = nc.dram_tensor("wu", [ICH, 128, KCH * 128], F32, kind="ExternalInput").ap()
    wd = nc.dram_tensor("wd", [OCH, 128, ICH * 128], F32, kind="ExternalInput").ap()
    bq_in = nc.dram_tensor("bq", [128, OCH], F32, kind="ExternalInput").ap()
    bk_in = nc.dram_tensor("bk", [128, OCH], F32, kind="ExternalInput").ap()
    bv_in = nc.dram_tensor("bv", [128, OCH], F32, kind="ExternalInput").ap()
    bo_in = nc.dram_tensor("bo_bcast", [128, H], F32, kind="ExternalInput").ap()
    gs_in = nc.dram_tensor("gs", [128, ICH], F32, kind="ExternalInput").ap()
    us_in = nc.dram_tensor("us", [128, ICH], F32, kind="ExternalInput").ap()
    ds_in = nc.dram_tensor("ds", [128, OCH], F32, kind="ExternalInput").ap()
    cosq_in = nc.dram_tensor("cosq", [128, TPC], BF16, kind="ExternalInput").ap()
    sinq_in = nc.dram_tensor("sinq", [128, TPC], BF16, kind="ExternalInput").ap()
    cosk_in = nc.dram_tensor("cosk", [128, TPC], BF16, kind="ExternalInput").ap()
    sink_in = nc.dram_tensor("sink", [128, TPC], BF16, kind="ExternalInput").ap()
    ident_in = nc.dram_tensor("ident", [128, 128], BF16, kind="ExternalInput").ap()
    identf_in = nc.dram_tensor("identf", [128, 128], F32, kind="ExternalInput").ap()
    rswap_in = nc.dram_tensor("rswap", [128, 128], BF16, kind="ExternalInput").ap()
    mask_in = nc.dram_tensor("mask", [128, 512], F32, kind="ExternalInput").ap()
    y_out = nc.dram_tensor("y", [H, TPC], F32, kind="ExternalOutput").ap()

    x2_dram = nc.dram_tensor("x2_dram", [TPC, H], F32)
    x2t_dram = nc.dram_tensor("x2t_dram", [H, TPC], F32)
    cck_in = nc.dram_tensor("cck_in", [H, TPC], BF16)
    ccv_in = nc.dram_tensor("ccv_in", [H, TPC], BF16)
    cck_out = nc.dram_tensor("cck_out", [4, 4 * 512, TPC], BF16)
    ccv_out = nc.dram_tensor("ccv_out", [4, 4 * 512, TPC], BF16)

    with contextlib.ExitStack() as outer:
        e = outer.enter_context
        sems = {n: e(nc.semaphore("sem_" + n)) for n in Prog.ENGINES}
        counts = {n: 0 for n in Prog.ENGINES}

        ident = e(nc.sbuf_tensor("ident_sb", [128, 128], BF16))
        identf = e(nc.sbuf_tensor("identf_sb", [128, 128], F32))
        rsw = e(nc.sbuf_tensor("rsw_sb", [128, 128], BF16))
        mask = e(nc.sbuf_tensor("mask_sb", [128, 512], F32))
        cosq = e(nc.sbuf_tensor("cosq_sb", [128, TPC], BF16))
        sinq = e(nc.sbuf_tensor("sinq_sb", [128, TPC], BF16))
        cosk = e(nc.sbuf_tensor("cosk_sb", [128, TPC], BF16))
        sink = e(nc.sbuf_tensor("sink_sb", [128, TPC], BF16))
        bo_sb = e(nc.sbuf_tensor("bo_sb", [128, H], F32))
        bq_sb = e(nc.sbuf_tensor("bq_sb", [128, OCH], F32))
        bk_sb = e(nc.sbuf_tensor("bk_sb", [128, OCH], F32))
        bv_sb = e(nc.sbuf_tensor("bv_sb", [128, OCH], F32))
        gs_sb = e(nc.sbuf_tensor("gs_sb", [128, ICH], F32))
        us_sb = e(nc.sbuf_tensor("us_sb", [128, ICH], F32))
        ds_sb = e(nc.sbuf_tensor("ds_sb", [128, OCH], F32))
        wst = [e(nc.sbuf_tensor(f"wstage{i}", [128, 2048], F32)) for i in range(2)]
        WQUEUE = ("sync", "scalar")
        wbf = [e(nc.sbuf_tensor(f"wbf{i}", [128, 2048], BF16)) for i in range(2)]
        wbf2 = [e(nc.sbuf_tensor(f"wbf2_{i}", [128, 2048], BF16)) for i in range(2)]
        st1 = e(nc.sbuf_tensor("st1", [128, 16], F32))
        st2 = e(nc.sbuf_tensor("st2", [128, 16], F32))
        st3 = e(nc.sbuf_tensor("st3", [128, 16], F32))
        st4 = e(nc.sbuf_tensor("st4", [128, 16], F32))

        with contextlib.ExitStack() as sc_at:
            a_t = sc_at.enter_context(nc.sbuf_tensor("a_t", [128, NH * TPC], BF16))

            with contextlib.ExitStack() as sc_qt:
                qt_sb = sc_qt.enter_context(nc.sbuf_tensor("qt_sb", [128, NH * TPC], BF16))

                with contextlib.ExitStack() as sa:
                    ea = sa.enter_context
                    xres = [ea(nc.sbuf_tensor(f"xres{i}", [128, H], F32)) for i in range(2)]
                    xn_tok = ea(nc.sbuf_tensor("xn_tok", [128, H], BF16))
                    sq_scr = ea(nc.sbuf_tensor("sq_scr", [128, H], F32))
                    xn_t = ea(nc.sbuf_tensor("xn_t", [128, KCH * TPC], BF16))
                    qk_f = ea(nc.sbuf_tensor("qk_f", [128, TPC], BF16))
                    rp1 = ea(nc.sbuf_tensor("rp1", [128, TPC], BF16))
                    rp2 = ea(nc.sbuf_tensor("rp2", [128, TPC], BF16))
                    kv_st = ea(nc.sbuf_tensor("kv_st", [128, TPC], BF16))
                    ps_tr = ea(nc.psum_tensor("ps_tr", [128, 1024], BF16))
                    ps_p = [ea(nc.psum_tensor(f"ps_p{i}", [128, 512], F32)) for i in range(2)]
                    ps_rope = ea(nc.psum_tensor("ps_rope", [128, 1024], F32))

                    p = Prog(sems, counts)
                    AD = {}

                    def _cl(dst, src):
                        return p.add("gpsimd", lambda g, d=dst, s=src: g.dma_start(out=d, in_=s),
                                     mark=True, dma=True)

                    _cl(ident[:], ident_in)
                    _cl(identf[:], identf_in)
                    _cl(rsw[:], rswap_in)
                    _cl(mask[:], mask_in)
                    _cl(cosq[:], cosq_in)
                    _cl(sinq[:], sinq_in)
                    _cl(cosk[:], cosk_in)
                    _cl(sink[:], sink_in)
                    _cl(bo_sb[:], bo_in)
                    _cl(bq_sb[:], bq_in)
                    _cl(bk_sb[:], bk_in)
                    _cl(bv_sb[:], bv_in)
                    _cl(gs_sb[:], gs_in)
                    _cl(us_sb[:], us_in)
                    ev_consts = _cl(ds_sb[:], ds_in)

                    ev_evac = {}
                    tr_last = {}
                    for t in range(8):
                        xb = xres[t % 2]
                        ev_x = p.add("sync", lambda s, t=t, xb=xb: s.dma_start(
                            out=xb[:], in_=x_in[t * 128:(t + 1) * 128, :]),
                            wait=(AD.get(("xres_free", t % 2)),), mark=True, dma=True)
                        ev_sum = p.add("vector", lambda v, t=t, xb=xb: v.reduce_sum(
                            st1[:, t:t + 1], xb[:], axis=AX.X), wait=(ev_x,), mark=True)
                        ev_sq = p.add("scalar", lambda s, t=t, xb=xb: s.activation(
                            sq_scr[:], xb[:], AF.Square, accum_out=st2[:, t:t + 1]),
                            wait=(ev_x,), mark=True)
                        ev1 = p.add("vector", lambda v, t=t: v.tensor_scalar(
                            st1[:, t:t + 1], st1[:, t:t + 1], 1.0 / H, None, ALU.mult),
                            wait=(ev_sum,), mark=True)
                        ev2 = p.add("vector", lambda v, t=t: v.tensor_scalar(
                            st2[:, t:t + 1], st2[:, t:t + 1], 1.0 / H, None, ALU.mult),
                            wait=(ev_sq,), mark=True)
                        ev3 = p.add("vector", lambda v, t=t: v.tensor_mul(
                            st3[:, t:t + 1], st1[:, t:t + 1], st1[:, t:t + 1]),
                            wait=(ev1,), mark=True)
                        ev4 = p.add("vector", lambda v, t=t: v.tensor_sub(
                            st2[:, t:t + 1], st2[:, t:t + 1], st3[:, t:t + 1]),
                            wait=(ev2, ev3), mark=True)
                        ev5 = p.add("vector", lambda v, t=t: v.tensor_scalar(
                            st2[:, t:t + 1], st2[:, t:t + 1], EPS, None, ALU.add),
                            wait=(ev4,), mark=True)
                        ev6 = p.add("vector", lambda v, t=t: v.reciprocal(
                            st3[:, t:t + 1], st2[:, t:t + 1]), wait=(ev5,), mark=True)
                        ev7 = p.add("scalar", lambda s, t=t: s.activation(
                            st2[:, t:t + 1], st3[:, t:t + 1], AF.Sqrt),
                            wait=(ev6,), mark=True)
                        ev8 = p.add("vector", lambda v, t=t: v.tensor_mul(
                            st3[:, t:t + 1], st1[:, t:t + 1], st2[:, t:t + 1]),
                            wait=(ev7,), mark=True)
                        ev9 = p.add("vector", lambda v, t=t: v.tensor_scalar(
                            st3[:, t:t + 1], st3[:, t:t + 1], -1.0, None, ALU.mult),
                            wait=(ev8,), mark=True)
                        ev_norm = p.add("scalar", lambda s, t=t, xb=xb: s.activation(
                            xn_tok[:], xb[:], AF.Identity,
                            bias=st3[:, t:t + 1], scale=st2[:, t:t + 1]),
                            wait=(ev9, tr_last.get(t - 1)), mark=True)
                        AD[("xres_free", t % 2)] = ev_norm
                        for half in range(2):
                            ev_tr = None
                            for kk in range(8):
                                kc = half * 8 + kk
                                ev_tr = p.add("tensor", lambda te, kc=kc, kk=kk: te.transpose(
                                    ps_tr[:, kk * 128:(kk + 1) * 128],
                                    xn_tok[:, kc * 128:(kc + 1) * 128], ident[:]),
                                    wait=(ev_norm, ev_consts, AD.get(("pstr_free",))),
                                    mark=(kk == 7))
                            ev_ev = p.add("vector", lambda v, t=t, half=half: v.tensor_copy(
                                xn_t[:].rearrange("q (kc tt) -> q kc tt", kc=KCH)[
                                    :, half * 8:(half + 1) * 8, t * 128:(t + 1) * 128],
                                ps_tr[:].rearrange("q (kk r) -> q kk r", kk=8)),
                                wait=(ev_tr,), mark=True)
                            AD[("pstr_free",)] = ev_ev
                            ev_evac[(t, half)] = ev_ev
                            if half == 1:
                                tr_last[t] = ev_tr
                    ev_xnt = ev_evac[(7, 1)]

                    def proj(name, wsrc, bias_sb, kind):
                        ev_last = None
                        for oc in range(OCH):
                            buf = wbf[oc % 2]
                            ws = wst[oc % 2]
                            ev_w = p.add(WQUEUE[oc % 2], lambda s, wsrc=wsrc, oc=oc, ws=ws: s.dma_start(
                                out=ws[:], in_=wsrc[oc]),
                                wait=(AD.get(("wst_free", oc % 2)),), mark=True, dma=True)
                            ev_cast = p.add("vector", lambda v, buf=buf, ws=ws: v.tensor_copy(
                                buf[:], ws[:]),
                                wait=(ev_w, AD.get(("wbf_free", oc % 2))), mark=True)
                            AD[("wst_free", oc % 2)] = ev_cast
                            ev_mm = [None, None]
                            for t2 in range(2):
                                for kc in range(KCH):
                                    ev_mm[t2] = p.add("tensor",
                                        lambda te, buf=buf, kc=kc, t2=t2: te.matmul(
                                            ps_p[t2][:], buf[:, kc * 128:(kc + 1) * 128],
                                            xn_t[:, kc * TPC + t2 * 512: kc * TPC + (t2 + 1) * 512],
                                            start=(kc == 0), stop=(kc == KCH - 1)),
                                        wait=(ev_cast, ev_xnt, AD.get(("psp_free", t2))),
                                        mark=(kc == KCH - 1))
                            AD[("wbf_free", oc % 2)] = ev_mm[1]
                            ev_ev = [None, None]
                            dst = kv_st if kind == "v" else qk_f
                            for t2 in range(2):
                                ev_ev[t2] = p.add("scalar", lambda s, t2=t2, dst=dst, oc=oc: s.activation(
                                    dst[:, t2 * 512:(t2 + 1) * 512], ps_p[t2][:], AF.Identity,
                                    bias=bias_sb[:, oc:oc + 1]),
                                    wait=(ev_mm[t2], ev_consts,
                                          AD.get(("dst_free", kind))), mark=True)
                                AD[("psp_free", t2)] = ev_ev[t2]
                            if kind == "v":
                                ev_last = p.add("gpsimd", lambda g, oc=oc: g.dma_start(
                                    out=ccv_in.ap()[oc * 128:(oc + 1) * 128, :], in_=kv_st[:]),
                                    wait=(ev_ev[1],), mark=True, dma=True)
                                AD[("dst_free", kind)] = ev_last
                                if oc % 4 == 3:
                                    g4 = oc // 4
                                    ccev[("v", g4)] = p.add("gpsimd",
                                        lambda g, g4=g4: g.collective_compute(
                                            "AllGather", ALU.bypass, replica_groups=GROUPS,
                                            ins=[ccv_in.ap()[512 * g4:512 * (g4 + 1), :]],
                                            outs=[ccv_out.ap()[g4]]),
                                        wait=(ev_last,), mark=True)
                            else:
                                ct = cosq if kind == "q" else cosk
                                st_ = sinq if kind == "q" else sink
                                dst2 = (qt_sb[:, oc * TPC:(oc + 1) * TPC] if kind == "q"
                                        else kv_st[:])
                                evr1 = p.add("vector", lambda v, ct=ct: v.tensor_mul(
                                    rp1[:], qk_f[:], ct[:]),
                                    wait=(ev_ev[0], ev_ev[1], AD.get(("rp_free",))), mark=True)
                                evs0 = p.add("tensor", lambda te: te.matmul(
                                    ps_rope[:, 0:512], rsw[:], qk_f[:, 0:512],
                                    start=True, stop=True),
                                    wait=(ev_ev[0], ev_consts,
                                          AD.get(("psrope_free",))), mark=True)
                                evs1 = p.add("tensor", lambda te: te.matmul(
                                    ps_rope[:, 512:1024], rsw[:], qk_f[:, 512:1024],
                                    start=True, stop=True),
                                    wait=(ev_ev[1],), mark=True)
                                evr23 = p.add("vector", lambda v, st_=st_: v.tensor_mul(
                                    rp2[:], ps_rope[:], st_[:]),
                                    wait=(evs0, evs1, AD.get(("rp2_free",))), mark=True)
                                AD[("dst_free", kind)] = evr23
                                AD[("psrope_free",)] = evr23
                                evr4 = p.add("vector", lambda v, dst2=dst2: v.tensor_add(
                                    dst2, rp1[:], rp2[:]),
                                    wait=(evr1, evr23,
                                          AD.get(("kvst_free",)) if kind == "k" else None),
                                    mark=True)
                                AD[("rp_free",)] = evr4
                                AD[("rp2_free",)] = evr4
                                if kind == "q":
                                    qrope_evs[oc] = evr4
                                if kind == "k":
                                    ev_last = p.add("gpsimd", lambda g, oc=oc: g.dma_start(
                                        out=cck_in.ap()[oc * 128:(oc + 1) * 128, :], in_=kv_st[:]),
                                        wait=(evr4,), mark=True, dma=True)
                                    AD[("kvst_free",)] = ev_last
                                    if oc % 4 == 3:
                                        g4 = oc // 4
                                        ccev[("k", g4)] = p.add("gpsimd",
                                            lambda g, g4=g4: g.collective_compute(
                                                "AllGather", ALU.bypass, replica_groups=GROUPS,
                                                ins=[cck_in.ap()[512 * g4:512 * (g4 + 1), :]],
                                                outs=[cck_out.ap()[g4]]),
                                            wait=(ev_last,), mark=True)
                        return ev_last

                    qrope_evs = {}
                    ccev = {}
                    ev_kd = proj("k", wk, bk_sb, "k")
                    AD[("dst_free", "v")] = ev_kd
                    ev_vd = proj("v", wv, bv_sb, "v")
                    proj("q", wq, bq_sb, "q")

                    blkA = sa.enter_context(nc.Block())
                    p.emit(blkA)
                    counts = dict(p.counts)

                with contextlib.ExitStack() as sb:
                    eb = sb.enter_context
                    poff = [0] * 9
                    for qi in range(8):
                        poff[qi + 1] = poff[qi] + (8 - qi) * 512
                    pp = eb(nc.sbuf_tensor("pp", [128, poff[8]], BF16))
                    kt_h = [eb(nc.sbuf_tensor(f"kt{i}", [128, 4096], BF16)) for i in range(2)]
                    vt_h = [eb(nc.sbuf_tensor(f"vt{i}", [128, 4096], BF16)) for i in range(2)]
                    vtok = eb(nc.sbuf_tensor("vtok", [128, 4096], BF16))
                    ptb = [eb(nc.sbuf_tensor(f"ptb{i}", [128, 1024], BF16)) for i in range(2)]
                    nm_sb = [eb(nc.sbuf_tensor(f"nm{i}", [128, 64], F32)) for i in range(2)]
                    sc_sb = [eb(nc.sbuf_tensor(f"sc{i}", [128, 64], F32)) for i in range(2)]
                    f_sb = eb(nc.sbuf_tensor("f_sb", [128, 64], F32))
                    g_sb = eb(nc.sbuf_tensor("g_sb", [128, 64], F32))
                    sf_sb = eb(nc.sbuf_tensor("sf_sb", [128, 64], F32))
                    nmg_sb = eb(nc.sbuf_tensor("nmg_sb", [128, 8], F32))
                    l_sb = eb(nc.sbuf_tensor("l_sb", [128, 8], F32))
                    r_sb = eb(nc.sbuf_tensor("r_sb", [128, 8], F32))
                    ps_s = [eb(nc.psum_tensor(f"ps_s{i}", [128, 512], F32)) for i in range(3)]
                    ps_pt = eb(nc.psum_tensor("ps_pt", [128, 1024], BF16))
                    ps_o = [eb(nc.psum_tensor(f"ps_o{i}", [128, 512], F32)) for i in range(2)]
                    ps_vt = eb(nc.psum_tensor("ps_vt", [128, 1024], BF16))

                    p = Prog(sems, counts, barrier=("tensor", "vector", "scalar"))
                    AD = {}
                    sslot = [0]

                    for h in range(NH):
                        kb_ = kt_h[h % 2]
                        vb_ = vt_h[h % 2]
                        ev_kl = ev_vl = None
                        for c in range(8):
                            for r in range(4):
                                ev_kl = p.add("sync", lambda s, c=c, r=r, kb_=kb_, h=h: s.dma_start(
                                    out=kb_[:, (c * 4 + r) * 128:(c * 4 + r + 1) * 128],
                                    in_=cck_out.ap()[h // 4][
                                        512 * r + 128 * (h % 4): 512 * r + 128 * (h % 4 + 1),
                                        128 * (7 - c):128 * (8 - c)]),
                                    wait=(ccev[("k", h // 4)], AD.get(("kbuf_free", h % 2)),),
                                    mark=True, dma=True)
                        for c in range(8):
                            for r in range(4):
                                ev_vl = p.add("sync", lambda s, c=c, r=r, vb_=vb_, h=h: s.dma_start(
                                    out=vb_[:, (c * 4 + r) * 128:(c * 4 + r + 1) * 128],
                                    in_=ccv_out.ap()[h // 4][
                                        512 * r + 128 * (h % 4): 512 * r + 128 * (h % 4 + 1),
                                        128 * (7 - c):128 * (8 - c)]),
                                    wait=(ccev[("v", h // 4)], AD.get(("vbuf_free", h % 2)),
                                          ev_vd if h == 1 else None),
                                    mark=True, dma=True)

                        ev_vtok = None
                        for c8 in range(4):
                            ev_tr = None
                            for kk in range(8):
                                kbi = c8 * 8 + kk
                                ev_tr = p.add("tensor", lambda te, kbi=kbi, kk=kk, vb_=vb_: te.transpose(
                                    ps_vt[:, kk * 128:(kk + 1) * 128],
                                    vb_[:, kbi * 128:(kbi + 1) * 128], ident[:]),
                                    wait=(ev_vl, AD.get(("psvt_free",)),
                                          AD.get(("vtok_free",))),
                                    mark=(kk == 7))
                            ev_vtok = p.add("vector", lambda v, c8=c8: v.tensor_copy(
                                vtok[:, c8 * 1024:(c8 + 1) * 1024], ps_vt[:]),
                                wait=(ev_tr,), mark=True)
                            AD[("psvt_free",)] = ev_vtok
                        AD[("vbuf_free", h % 2)] = Ev("tensor", p.counts["tensor"])

                        nm = nm_sb[h % 2]
                        sc = sc_sb[h % 2]
                        ev_exp = {}
                        ev_qk_last = None
                        ev_pscale = {}
                        for c in range(8):
                            for qi in range(8 - c):
                                sl = sslot[0] % 3
                                sslot[0] += 1
                                psb = ps_s[sl]
                                ev_mm = p.add("tensor", lambda te, h=h, qi=qi, c=c, psb=psb, kb_=kb_: te.matmul(
                                    psb[:], qt_sb[:, h * TPC + qi * 128: h * TPC + (qi + 1) * 128],
                                    kb_[:, c * 512:(c + 1) * 512], start=True, stop=True),
                                    wait=(ev_kl, AD.get(("ps_s_free", sl)),
                                          qrope_evs[h] if (qi == 0 and c == 0) else None),
                                    mark=True)
                                ev_qk_last = ev_mm
                                if qi == 7 - c:
                                    ev_mm = p.add("vector", lambda v, psb=psb: v.tensor_add(
                                        psb[:], psb[:], mask[:]),
                                        wait=(ev_mm, ev_consts), mark=True)
                                ev_mx = p.add("vector", lambda v, psb=psb, qi=qi, c=c, nm=nm: v.reduce_max(
                                    nm[:, qi * 8 + c: qi * 8 + c + 1], psb[:], axis=AX.X,
                                    negate=True),
                                    wait=(ev_mm, AD.get(("nm_free", h % 2, qi)),), mark=True)
                                ev_e = p.add("scalar", lambda s, psb=psb, qi=qi, c=c, nm=nm, sc=sc: s.activation(
                                    pp[:, poff[qi] + c * 512: poff[qi] + (c + 1) * 512],
                                    psb[:], AF.Exp, bias=nm[:, qi * 8 + c: qi * 8 + c + 1],
                                    accum_out=sc[:, qi * 8 + c: qi * 8 + c + 1]),
                                    wait=(ev_mx, AD.get(("pp_free", h, qi, c)),
                                          AD.get(("sc_free", h % 2, qi))), mark=True)
                                ev_exp[(qi, c)] = ev_e
                                AD[("ps_s_free", sl)] = ev_e

                                if qi == 7 - c:
                                    cnt = 8 - qi
                                    evc1 = p.add("vector", lambda v, qi=qi, cnt=cnt, nm=nm: v.tensor_reduce(
                                        nmg_sb[:, qi:qi + 1], nm[:, qi * 8: qi * 8 + cnt],
                                        axis=AX.X, op=ALU.min),
                                        wait=(ev_mx, AD.get(("nmg_free", qi))), mark=True)
                                    evc2 = p.add("scalar", lambda s, qi=qi, cnt=cnt, nm=nm: s.activation(
                                        f_sb[:, qi * 8: qi * 8 + cnt],
                                        nm[:, qi * 8: qi * 8 + cnt], AF.Exp,
                                        bias=nmg_sb[:, qi:qi + 1], scale=-1.0),
                                        wait=(evc1, AD.get(("f_free", qi))), mark=True)
                                    evc3 = p.add("vector", lambda v, qi=qi, cnt=cnt, sc=sc: v.tensor_mul(
                                        sf_sb[:, qi * 8: qi * 8 + cnt],
                                        sc[:, qi * 8: qi * 8 + cnt],
                                        f_sb[:, qi * 8: qi * 8 + cnt]),
                                        wait=(evc2, ev_e), mark=True)
                                    evc4 = p.add("vector", lambda v, qi=qi, cnt=cnt: v.reduce_sum(
                                        l_sb[:, qi:qi + 1], sf_sb[:, qi * 8: qi * 8 + cnt],
                                        axis=AX.X), wait=(evc3,), mark=True)
                                    evc5 = p.add("vector", lambda v, qi=qi: v.reciprocal(
                                        r_sb[:, qi:qi + 1], l_sb[:, qi:qi + 1]),
                                        wait=(evc4,), mark=True)
                                    evc6 = p.add("vector", lambda v, qi=qi, cnt=cnt: v.tensor_scalar(
                                        g_sb[:, qi * 8: qi * 8 + cnt],
                                        f_sb[:, qi * 8: qi * 8 + cnt],
                                        r_sb[:, qi:qi + 1], None, ALU.mult),
                                        wait=(evc5,), mark=True)
                                    AD[("nm_free", h % 2, qi)] = evc2
                                    AD[("sc_free", h % 2, qi)] = evc3
                                    AD[("nmg_free", qi)] = evc2
                                    AD[("f_free", qi)] = evc6
                                    evs = None
                                    for c2 in range(cnt):
                                        evs = p.add("vector", lambda v, qi=qi, c2=c2: v.tensor_scalar(
                                            pp[:, poff[qi] + c2 * 512: poff[qi] + (c2 + 1) * 512],
                                            pp[:, poff[qi] + c2 * 512: poff[qi] + (c2 + 1) * 512],
                                            g_sb[:, qi * 8 + c2: qi * 8 + c2 + 1], None, ALU.mult),
                                            wait=(evc6, ev_exp[(qi, c2)]),
                                            mark=(c2 == cnt - 1))
                                    ev_pscale[qi] = evs

                        ev_pv_last = None
                        ev_pt_last = None
                        for c in range(8):
                            pref = 8 - c
                            for kbi in range(4):
                                pb = ptb[(c * 4 + kbi) % 2]
                                ev_tr = None
                                for qi in range(pref):
                                    ev_tr = p.add("tensor", lambda te, qi=qi, c=c, kbi=kbi: te.transpose(
                                        ps_pt[:, qi * 128:(qi + 1) * 128],
                                        pp[:, poff[qi] + c * 512 + kbi * 128:
                                           poff[qi] + c * 512 + (kbi + 1) * 128], ident[:]),
                                        wait=(ev_pscale[qi], AD.get(("pspt_free",))),
                                        mark=(qi == pref - 1))
                                ev_ptev = p.add("vector", lambda v, pref=pref, pb=pb: v.tensor_copy(
                                    pb[:, 0:pref * 128], ps_pt[:, 0:pref * 128]),
                                    wait=(ev_tr, AD.get(("ptb_free", (c * 4 + kbi) % 2))),
                                    mark=True)
                                AD[("pspt_free",)] = ev_ptev
                                ev_pt_last = ev_ptev
                                kbg = c * 4 + kbi
                                n0 = min(512, pref * 128)
                                ev_pv = p.add("tensor", lambda te, kbg=kbg, n0=n0, pb=pb, c=c: te.matmul(
                                    ps_o[0][:, 0:n0], vtok[:, kbg * 128:(kbg + 1) * 128],
                                    pb[:, 0:n0], start=(c == 0 and kbg == 0),
                                    stop=(c == 7 and kbg == 31)),
                                    wait=(ev_vtok, ev_ptev, AD.get(("ps_o_free", 0))), mark=True)
                                if pref > 4:
                                    n1 = (pref - 4) * 128
                                    ev_pv = p.add("tensor", lambda te, kbg=kbg, n1=n1, pb=pb, c=c: te.matmul(
                                        ps_o[1][:, 0:n1], vtok[:, kbg * 128:(kbg + 1) * 128],
                                        pb[:, 512:512 + n1], start=(c == 0 and kbg == 0),
                                        stop=(c == 3 and kbg == 15)),
                                        wait=(AD.get(("ps_o_free", 1)),), mark=True)
                                ev_pv_last = ev_pv
                                AD[("ptb_free", (c * 4 + kbi) % 2)] = ev_pv
                        AD[("pp_free_src",)] = ev_pt_last
                        for qi in range(8):
                            for c in range(8 - qi):
                                AD[("pp_free", h + 1, qi, c)] = ev_pt_last
                        ev_ao0 = p.add("vector", lambda v, h=h: v.tensor_copy(
                            a_t[:, h * TPC: h * TPC + 512], ps_o[0][:]),
                            wait=(ev_pv_last,), mark=True)
                        ev_ao1 = p.add("vector", lambda v, h=h: v.tensor_copy(
                            a_t[:, h * TPC + 512: h * TPC + 1024], ps_o[1][:]),
                            wait=(ev_pv_last,), mark=True)
                        AD[("ps_o_free", 0)] = ev_ao0
                        AD[("ps_o_free", 1)] = ev_ao1
                        AD[("kbuf_free", h % 2)] = ev_qk_last
                        AD[("vtok_free",)] = ev_pv_last

                    blkB = sb.enter_context(nc.Block())
                    p.emit(blkB)
                    counts = dict(p.counts)

            with contextlib.ExitStack() as s2:
                e2 = s2.enter_context
                wom = e2(nc.sbuf_tensor("wom", [128, KCH * 2048], BF16))
                xstr = [e2(nc.sbuf_tensor(f"xstr{i}", [128, H], F32)) for i in range(2)]
                xa = [e2(nc.sbuf_tensor(f"xa{i}", [128, H], F32)) for i in range(2)]
                xat = [e2(nc.sbuf_tensor(f"xat{i}", [128, H], F32)) for i in range(2)]
                ps_y = [e2(nc.psum_tensor(f"ps_y{i}", [128, 512], F32)) for i in range(4)]
                ps_t2 = e2(nc.psum_tensor("ps_t2", [128, 512], F32))

                p = Prog(sems, counts, barrier=True)
                AD = {}
                x2_ev = {}
                x2t_ev = {}
                ev_cast = {}
                for fc in range(KCH):
                    ws = wst[fc % 2]
                    ev_w = p.add(WQUEUE[fc % 2], lambda s, fc=fc, ws=ws: s.dma_start(
                        out=ws[:], in_=wo[fc]),
                        wait=(AD.get(("wst_free", fc % 2)),), mark=True, dma=True)
                    ev_cast[fc] = p.add("vector", lambda v, fc=fc, ws=ws: v.tensor_copy(
                        wom[:, fc * 2048:(fc + 1) * 2048], ws[:]),
                        wait=(ev_w,), mark=True)
                    AD[("wst_free", fc % 2)] = ev_cast[fc]
                for tq in range(8):
                    xb = xstr[tq % 2]
                    ev_x = p.add("gpsimd", lambda g, tq=tq, xb=xb: g.dma_start(
                        out=xb[:], in_=x_in[tq * 128:(tq + 1) * 128, :]),
                        wait=(AD.get(("xstr_free", tq % 2)),), mark=True, dma=True)
                    ev_mm = [None] * 4
                    for fc in range(KCH):
                        for o4 in range(4):
                            ev_mm[o4] = p.add("tensor", lambda te, tq=tq, fc=fc, o4=o4: te.matmul(
                                ps_y[o4][:], a_t[:, fc * TPC + tq * 128: fc * TPC + (tq + 1) * 128],
                                wom[:, fc * 2048 + o4 * 512: fc * 2048 + (o4 + 1) * 512],
                                start=(fc == 0), stop=(fc == KCH - 1)),
                                wait=(ev_cast[fc], AD.get(("ps_y_free", o4))), mark=(fc == KCH - 1))
                    xab = xa[tq % 2]
                    ev_add = [None] * 4
                    for o4 in range(4):
                        ev_add[o4] = p.add("vector", lambda v, o4=o4, xab=xab, xb=xb: v.tensor_add(
                            xab[:, o4 * 512:(o4 + 1) * 512], ps_y[o4][:],
                            xb[:, o4 * 512:(o4 + 1) * 512]),
                            wait=(ev_mm[o4], ev_x, AD.get(("xa_free", tq % 2))), mark=True)
                        AD[("ps_y_free", o4)] = ev_add[o4]
                    AD[("xstr_free", tq % 2)] = ev_add[3]
                    ev_bo = p.add("vector", lambda v, xab=xab: v.tensor_add(
                        xab[:], xab[:], bo_sb[:]),
                        wait=(ev_add[0], ev_add[1], ev_add[2], ev_add[3]), mark=True)
                    ev_x2 = p.add("gpsimd", lambda g, tq=tq, xab=xab: g.dma_start(
                        out=x2_dram.ap()[tq * 128:(tq + 1) * 128, :], in_=xab[:]),
                        wait=(ev_bo,), mark=True, dma=True)
                    xatb = xat[tq % 2]
                    ev_x2t = None
                    for q4 in range(4):
                        ev_tr = None
                        for kk in range(4):
                            fc = q4 * 4 + kk
                            ev_tr = p.add("tensor", lambda te, fc=fc, kk=kk, xab=xab: te.transpose(
                                ps_t2[:, kk * 128:(kk + 1) * 128],
                                xab[:, fc * 128:(fc + 1) * 128],
                                identf[:]),
                                wait=(ev_bo, AD.get(("ps_t2_free",))), mark=(kk == 3))
                        ev_x2t = p.add("vector", lambda v, q4=q4, xatb=xatb: v.tensor_copy(
                            xatb[:, q4 * 512:(q4 + 1) * 512], ps_t2[:]),
                            wait=(ev_tr, AD.get(("xat_free", tq % 2))), mark=True)
                        AD[("ps_t2_free",)] = ev_x2t
                    ev_x2td = None
                    for fc in range(KCH):
                        ev_x2td = p.add("gpsimd", lambda g, tq=tq, fc=fc, xatb=xatb: g.dma_start(
                            out=x2t_dram.ap()[fc * 128:(fc + 1) * 128, tq * 128:(tq + 1) * 128],
                            in_=xatb[:, fc * 128:(fc + 1) * 128]),
                            wait=(ev_x2t,), mark=True, dma=True)
                    AD[("xa_free", tq % 2)] = ev_x2td
                    AD[("xat_free", tq % 2)] = ev_x2td
                    x2_ev[tq] = ev_x2
                    x2t_ev[tq] = ev_x2td

                blkB2 = s2.enter_context(nc.Block())
                p.emit(blkB2)
                counts = dict(p.counts)

        with contextlib.ExitStack() as sc:
            ec = sc.enter_context
            h_t = ec(nc.sbuf_tensor("h_t", [128, ICH * TPC], BF16))
            xstr2 = [ec(nc.sbuf_tensor(f"xstr2_{i}", [128, H], F32)) for i in range(2)]
            ps_g = ec(nc.psum_tensor("ps_g", [128, 1024], F32))
            ps_u = ec(nc.psum_tensor("ps_u", [128, 1024], F32))
            ps_tr3 = ec(nc.psum_tensor("ps_tr3", [128, 1024], BF16))
            ga_f = ec(nc.sbuf_tensor("ga_f", [128, 1024], F32))

            p = Prog(sems, counts, barrier=True)
            AD = {}

            with contextlib.ExitStack() as scz:
                z_t = scz.enter_context(nc.sbuf_tensor("z_t", [128, KCH * TPC], BF16))
                zn_tok = scz.enter_context(nc.sbuf_tensor("zn_tok", [128, H], BF16))
                sq2 = scz.enter_context(nc.sbuf_tensor("sq2", [128, H], F32))

                tr2_last = {}
                for t in range(8):
                    xb = xstr2[t % 2]
                    ev_x = p.add("gpsimd", lambda g, t=t, xb=xb: g.dma_start(
                        out=xb[:], in_=x2_dram.ap()[t * 128:(t + 1) * 128, :]),
                        wait=(AD.get(("x2_free", t % 2)), x2_ev[t]), mark=True, dma=True)
                    ev_sum = p.add("vector", lambda v, t=t, xb=xb: v.reduce_sum(
                        st1[:, t:t + 1], xb[:], axis=AX.X), wait=(ev_x,), mark=True)
                    ev_sq = p.add("scalar", lambda s, t=t, xb=xb: s.activation(
                        sq2[:], xb[:], AF.Square, accum_out=st2[:, t:t + 1]),
                        wait=(ev_x,), mark=True)
                    ev1 = p.add("vector", lambda v, t=t: v.tensor_scalar(
                        st1[:, t:t + 1], st1[:, t:t + 1], 1.0 / H, None, ALU.mult),
                        wait=(ev_sum,), mark=True)
                    ev2 = p.add("vector", lambda v, t=t: v.tensor_scalar(
                        st2[:, t:t + 1], st2[:, t:t + 1], 1.0 / H, None, ALU.mult),
                        wait=(ev_sq,), mark=True)
                    ev3 = p.add("vector", lambda v, t=t: v.tensor_mul(
                        st3[:, t:t + 1], st1[:, t:t + 1], st1[:, t:t + 1]),
                        wait=(ev1,), mark=True)
                    ev4 = p.add("vector", lambda v, t=t: v.tensor_sub(
                        st2[:, t:t + 1], st2[:, t:t + 1], st3[:, t:t + 1]),
                        wait=(ev2, ev3), mark=True)
                    ev5 = p.add("vector", lambda v, t=t: v.tensor_scalar(
                        st2[:, t:t + 1], st2[:, t:t + 1], EPS, None, ALU.add),
                        wait=(ev4,), mark=True)
                    ev6 = p.add("vector", lambda v, t=t: v.reciprocal(
                        st3[:, t:t + 1], st2[:, t:t + 1]), wait=(ev5,), mark=True)
                    ev7 = p.add("scalar", lambda s, t=t: s.activation(
                        st2[:, t:t + 1], st3[:, t:t + 1], AF.Sqrt),
                        wait=(ev6,), mark=True)
                    ev8 = p.add("vector", lambda v, t=t: v.tensor_mul(
                        st3[:, t:t + 1], st1[:, t:t + 1], st2[:, t:t + 1]),
                        wait=(ev7,), mark=True)
                    ev9 = p.add("vector", lambda v, t=t: v.tensor_scalar(
                        st3[:, t:t + 1], st3[:, t:t + 1], -1.0, None, ALU.mult),
                        wait=(ev8,), mark=True)
                    ev_norm = p.add("scalar", lambda s, t=t, xb=xb: s.activation(
                        zn_tok[:], xb[:], AF.Identity,
                        bias=st3[:, t:t + 1], scale=st2[:, t:t + 1]),
                        wait=(ev9, tr2_last.get(t - 1)), mark=True)
                    AD[("x2_free", t % 2)] = ev_norm
                    for half in range(2):
                        ev_tr = None
                        for kk in range(8):
                            kc = half * 8 + kk
                            ev_tr = p.add("tensor", lambda te, kc=kc, kk=kk: te.transpose(
                                ps_tr3[:, kk * 128:(kk + 1) * 128],
                                zn_tok[:, kc * 128:(kc + 1) * 128], ident[:]),
                                wait=(ev_norm, AD.get(("pstr3_free",))), mark=(kk == 7))
                        ev_ev = p.add("vector", lambda v, t=t, half=half: v.tensor_copy(
                            z_t[:].rearrange("q (kc tt) -> q kc tt", kc=KCH)[
                                :, half * 8:(half + 1) * 8, t * 128:(t + 1) * 128],
                            ps_tr3[:].rearrange("q (kk r) -> q kk r", kk=8)),
                            wait=(ev_tr,), mark=True)
                        AD[("pstr3_free",)] = ev_ev
                        if half == 1:
                            tr2_last[t] = ev_tr
                        if t == 7 and half == 1:
                            ev_znt = ev_ev

                for ic in range(ICH):
                    gb = wbf[ic % 2]
                    ub = wbf2[ic % 2]
                    ev_w1 = p.add(WQUEUE[0], lambda s, ic=ic: s.dma_start(
                        out=wst[0][:], in_=wg[ic]),
                        wait=(AD.get(("wst_free", 0)),), mark=True, dma=True)
                    ev_c1 = p.add("vector", lambda v, gb=gb: v.tensor_copy(
                        gb[:], wst[0][:]),
                        wait=(ev_w1, AD.get(("gb_free", ic % 2))), mark=True)
                    AD[("wst_free", 0)] = ev_c1
                    ev_w2 = p.add(WQUEUE[1], lambda s, ic=ic: s.dma_start(
                        out=wst[1][:], in_=wu[ic]),
                        wait=(AD.get(("wst_free", 1)),), mark=True, dma=True)
                    ev_c2 = p.add("vector", lambda v, ub=ub: v.tensor_copy(
                        ub[:], wst[1][:]),
                        wait=(ev_w2, AD.get(("ub_free", ic % 2))), mark=True)
                    AD[("wst_free", 1)] = ev_c2
                    ev_gmm = [None, None]
                    ev_umm = [None, None]
                    for t2 in range(2):
                        for kc in range(KCH):
                            ev_gmm[t2] = p.add("tensor", lambda te, gb=gb, kc=kc, t2=t2: te.matmul(
                                ps_g[:, t2 * 512:(t2 + 1) * 512],
                                gb[:, kc * 128:(kc + 1) * 128],
                                z_t[:, kc * TPC + t2 * 512: kc * TPC + (t2 + 1) * 512],
                                start=(kc == 0), stop=(kc == KCH - 1)),
                                wait=(ev_c1, ev_znt, AD.get(("ps_g_free",))),
                                mark=(kc == KCH - 1))
                    for t2 in range(2):
                        for kc in range(KCH):
                            ev_umm[t2] = p.add("tensor", lambda te, ub=ub, kc=kc, t2=t2: te.matmul(
                                ps_u[:, t2 * 512:(t2 + 1) * 512],
                                ub[:, kc * 128:(kc + 1) * 128],
                                z_t[:, kc * TPC + t2 * 512: kc * TPC + (t2 + 1) * 512],
                                start=(kc == 0), stop=(kc == KCH - 1)),
                                wait=(ev_c2, AD.get(("ps_u_free",))),
                                mark=(kc == KCH - 1))
                    AD[("gb_free", ic % 2)] = ev_gmm[1]
                    AD[("ub_free", ic % 2)] = ev_umm[1]
                    ev_silu = p.add("scalar", lambda s, ic=ic: s.activation(
                        ga_f[:], ps_g[:], AF.Silu, scale=gs_sb[:, ic:ic + 1]),
                        wait=(ev_gmm[0], ev_gmm[1], AD.get(("ga_free",))), mark=True)
                    AD[("ps_g_free",)] = ev_silu
                    ev_h = p.add("vector", lambda v, ic=ic: v.scalar_tensor_tensor(
                        h_t[:, ic * TPC:(ic + 1) * TPC], ps_u[:], us_sb[:, ic:ic + 1],
                        ga_f[:], ALU.mult, ALU.mult),
                        wait=(ev_umm[0], ev_umm[1], ev_silu), mark=True)
                    AD[("ps_u_free",)] = ev_h
                    AD[("ga_free",)] = ev_h
                    AD[("h_done", ic)] = ev_h
                last_zt_read = ev_umm[1]
                for i_ in range(2):
                    AD[("db_free", i_)] = last_zt_read
                    AD[("x2t_free", i_)] = last_zt_read
                    AD[("yt_free", i_)] = last_zt_read

            with contextlib.ExitStack() as scd:
                dbf = [scd.enter_context(nc.sbuf_tensor(f"dbf{i}", [128, ICH * 128], BF16))
                       for i in range(2)]
                x2t_sb = [scd.enter_context(nc.sbuf_tensor(f"x2t{i}", [128, TPC], F32))
                          for i in range(2)]
                yt_sb = [scd.enter_context(nc.sbuf_tensor(f"yt{i}", [128, TPC], F32))
                         for i in range(2)]
                ps_y2 = scd.enter_context(nc.psum_tensor("ps_dn", [128, 1024], F32))
                ev_ydma = None
                for oc in range(OCH):
                    db = dbf[oc % 2]
                    ev_cast = None
                    for piece in range(3):
                        lo = piece * 2048
                        hi = min(ICH * 128, lo + 2048)
                        ws = wst[piece % 2]
                        ev_w = p.add(WQUEUE[piece % 2], lambda s, oc=oc, lo=lo, hi=hi, ws=ws: s.dma_start(
                            out=ws[:, 0:hi - lo], in_=wd[oc][:, lo:hi]),
                            wait=(AD.get(("wst_free", piece % 2)),), mark=True, dma=True)
                        ev_cast = p.add("vector", lambda v, db=db, lo=lo, hi=hi, ws=ws: v.tensor_copy(
                            db[:, lo:hi], ws[:, 0:hi - lo]),
                            wait=(ev_w, AD.get(("db_free", oc % 2))), mark=True)
                        AD[("wst_free", piece % 2)] = ev_cast
                    xtb = x2t_sb[oc % 2]
                    ev_xt = p.add("gpsimd", lambda g, oc=oc, xtb=xtb: g.dma_start(
                        out=xtb[:], in_=x2t_dram.ap()[oc * 128:(oc + 1) * 128, :]),
                        wait=(AD.get(("x2t_free", oc % 2)), x2t_ev[7]), mark=True, dma=True)
                    ev_mm = [None, None]
                    for ic in range(ICH):
                        for t2 in range(2):
                            ev_mm[t2] = p.add("tensor", lambda te, db=db, ic=ic, t2=t2: te.matmul(
                                ps_y2[:, t2 * 512:(t2 + 1) * 512],
                                db[:, ic * 128:(ic + 1) * 128],
                                h_t[:, ic * TPC + t2 * 512: ic * TPC + (t2 + 1) * 512],
                                start=(ic == 0), stop=(ic == ICH - 1)),
                                wait=(ev_cast, AD[("h_done", ic)],
                                      AD.get(("ps_y2_free",))),
                                mark=(ic == ICH - 1))
                    AD[("db_free", oc % 2)] = ev_mm[1]
                    ytb = yt_sb[oc % 2]
                    ev_y = p.add("vector", lambda v, oc=oc, ytb=ytb, xtb=xtb: v.scalar_tensor_tensor(
                        ytb[:], ps_y2[:], ds_sb[:, oc:oc + 1], xtb[:],
                        ALU.mult, ALU.add),
                        wait=(ev_mm[0], ev_mm[1], ev_xt, AD.get(("yt_free", oc % 2))),
                        mark=True)
                    AD[("ps_y2_free",)] = ev_y
                    AD[("x2t_free", oc % 2)] = ev_y
                    ev_ydma = p.add("gpsimd", lambda g, oc=oc, ytb=ytb: g.dma_start(
                        out=y_out[oc * 128:(oc + 1) * 128, :], in_=ytb[:]),
                        wait=(ev_y,), mark=True, dma=True)
                    AD[("yt_free", oc % 2)] = ev_ydma
                p.add("gpsimd", lambda g, v=ev_ydma.val: g.wait_ge(sems["gpsimd"], v))

                blkC = sc.enter_context(nc.Block())
                p.emit(blkC)

    return nc


_NC_CACHE = [None]


def _rope_tables(pos, scale):
    half = D // 2
    inv_freq = 1.0 / (10000.0 ** (np.arange(half, dtype=np.float32) / half))
    ang = pos[None, :].astype(np.float32) * inv_freq[:, None]
    cos = np.cos(ang).astype(np.float32)
    sin = np.sin(ang).astype(np.float32)
    cosT = np.concatenate([cos, cos], axis=0) * scale
    sinT = np.concatenate([-sin, sin], axis=0) * scale
    return (cosT.astype(ml_dtypes.bfloat16), sinT.astype(ml_dtypes.bfloat16))


def kernel(L0_attn_x, L0_attn_Wq, L0_attn_bq, L0_attn_Wk, L0_attn_bk,
           L0_attn_Wv, L0_attn_bv, L0_attn_Wo, L0_attn_bo,
           L0_ffn_gate_w, L0_ffn_up_w, L0_ffn_down_w,
           L0_ffn_gate_s, L0_ffn_up_s, L0_ffn_down_s):
    x = np.ascontiguousarray(np.asarray(L0_attn_x, dtype=np.float32))
    Wq = np.asarray(L0_attn_Wq, dtype=np.float32)
    Wk = np.asarray(L0_attn_Wk, dtype=np.float32)
    Wv = np.asarray(L0_attn_Wv, dtype=np.float32)
    Wo = np.asarray(L0_attn_Wo, dtype=np.float32)
    bq = np.asarray(L0_attn_bq, dtype=np.float32)
    bk = np.asarray(L0_attn_bk, dtype=np.float32)
    bv = np.asarray(L0_attn_bv, dtype=np.float32)
    bo = np.asarray(L0_attn_bo, dtype=np.float32)
    gw = np.asarray(L0_ffn_gate_w, dtype=np.float32)
    uw = np.asarray(L0_ffn_up_w, dtype=np.float32)
    dw = np.asarray(L0_ffn_down_w, dtype=np.float32)
    gs = np.asarray(L0_ffn_gate_s, dtype=np.float32)
    us = np.asarray(L0_ffn_up_s, dtype=np.float32)
    ds = np.asarray(L0_ffn_down_s, dtype=np.float32)

    perm = np.concatenate([
        np.concatenate([h * D + 2 * np.arange(64), h * D + 2 * np.arange(64) + 1])
        for h in range(NH)]).astype(np.int64)

    def pack_lhsT(w_rows_out_cols_in, no, nk):
        a = w_rows_out_cols_in.reshape(no, 128, nk, 128)
        return np.ascontiguousarray(a.transpose(0, 3, 2, 1)).reshape(no, 128, nk * 128)

    wq_p = pack_lhsT(Wq[perm], OCH, KCH)
    wk_p = pack_lhsT(Wk[perm], OCH, KCH)
    wv_p = pack_lhsT(Wv, OCH, KCH)
    wg_p = pack_lhsT(gw, ICH, KCH)
    wu_p = pack_lhsT(uw, ICH, KCH)
    wd_p = pack_lhsT(dw, OCH, ICH)
    wo_p = np.ascontiguousarray(Wo.T.reshape(KCH, 128, H))

    bq_p = np.ascontiguousarray(bq[perm].reshape(OCH, 128).T)
    bk_p = np.ascontiguousarray(bk[perm].reshape(OCH, 128).T)
    bv_p = np.ascontiguousarray(bv.reshape(OCH, 128).T)
    gs_p = np.ascontiguousarray(gs.reshape(ICH, 128).T)
    us_p = np.ascontiguousarray(us.reshape(ICH, 128).T)
    ds_p = np.ascontiguousarray(ds.reshape(OCH, 128).T)
    bo_bcast = np.ascontiguousarray(np.broadcast_to(bo, (128, H)))
    ident = np.eye(128, dtype=ml_dtypes.bfloat16)
    identf = np.eye(128, dtype=np.float32)
    rswap = np.zeros((128, 128), dtype=ml_dtypes.bfloat16)
    for i in range(128):
        rswap[i, (i + 64) % 128] = 1

    if _NC_CACHE[0] is None:
        _NC_CACHE[0] = build_nc()
    nc = _NC_CACHE[0]

    in_maps = []
    shared = dict(wq=wq_p, wk=wk_p, wv=wv_p, wo=wo_p, wg=wg_p, wu=wu_p, wd=wd_p,
                  bq=bq_p, bk=bk_p, bv=bv_p, bo_bcast=bo_bcast,
                  gs=gs_p, us=us_p, ds=ds_p, ident=ident, identf=identf,
                  rswap=rswap)
    for core in range(8):
        b = core // 4
        j = core % 4
        gtiles = [4 * (7 - qi) + j for qi in range(8)]
        rows = np.concatenate([np.arange(g * 128, (g + 1) * 128) for g in gtiles])
        xc = np.ascontiguousarray(x[b].reshape(S, H)[rows])
        pos = rows.astype(np.float32)
        cq, sq = _rope_tables(pos, 1.0 / np.sqrt(np.float32(D)))
        ck, sk = _rope_tables(pos, 1.0)
        m = np.zeros((128, 512), dtype=np.float32)
        for pp_ in range(128):
            m[pp_, j * 128 + pp_ + 1:] = NEGM
        im = dict(shared)
        im.update(x=xc, cosq=cq, sinq=sq, cosk=ck, sink=sk,
                  mask=np.ascontiguousarray(m))
        in_maps.append(im)

    res = run_bass_kernel_spmd(nc, in_maps, core_ids=list(range(8)))

    out = np.empty((B, S, H), dtype=np.float32)
    for core in range(8):
        b = core // 4
        j = core % 4
        yc = res.results[core]["y"]
        for qi in range(8):
            g = 4 * (7 - qi) + j
            out[b, g * 128:(g + 1) * 128, :] = yc[:, qi * 128:(qi + 1) * 128].T
    return out
